# version 04 (final)
# baseline (speedup 1.0000x reference)
import numpy as np
import ml_dtypes
import concourse.bass as bass
import concourse.mybir as mybir
from concourse.bass_utils import run_bass_kernel_spmd

F32 = mybir.dt.float32
BF16 = mybir.dt.float16
BFR16 = mybir.dt.bfloat16
I32 = mybir.dt.int32
U8 = mybir.dt.uint8
AF = mybir.ActivationFunctionType
ALU = mybir.AluOpType
AX = mybir.AxisListType
BF = np.float16
BFR = ml_dtypes.bfloat16

NCORE = 8
D = 1024
S = 2048
BPC = 2
T = BPC * S
NT = T // 128
NTS = S // 128
KD = D // 128

SBUF_TOP = 229344


ENGS = ("pe", "act", "dve", "pool", "sp")
DMAQ = {"sp": "d_sp", "act": "d_act", "pool": "d_pool"}
VKS = ENGS + ("d_sp", "d_act", "d_pool")

WHOLE = (0, 1 << 60)


class Ev:
    __slots__ = ("vk", "n")

    def __init__(self, vk, n):
        self.vk = vk
        self.n = n


class Rec:
    __slots__ = ("fn", "waits", "ev", "marked")

    def __init__(self, fn, waits, ev):
        self.fn = fn
        self.waits = waits
        self.ev = ev
        self.marked = False


class Buf:
    __slots__ = ("name", "recs")

    def __init__(self, name):
        self.name = name
        self.recs = []

    def rdeps(self, lo, hi):
        return [ev for (l, h, k, ev) in self.recs if k == "w" and l < hi and lo < h]

    def wdeps(self, lo, hi):
        return [ev for (l, h, k, ev) in self.recs if l < hi and lo < h]

    def note_read(self, lo, hi, ev):
        for i, (l, h, k, e) in enumerate(self.recs):
            if k == "r" and l == lo and h == hi and e.vk == ev.vk:
                self.recs[i] = (l, h, k, ev)
                return
        self.recs.append((lo, hi, "r", ev))

    def note_write(self, lo, hi, ev):
        self.recs = [r for r in self.recs if not (r[0] >= lo and r[1] <= hi)]
        self.recs.append((lo, hi, "w", ev))


def _norm(items):
    out = []
    for it in items:
        if isinstance(it, Buf):
            out.append((it, WHOLE[0], WHOLE[1]))
        else:
            b, lo, hi = it
            out.append((b, lo, hi))
    return out


class Prog:
    def __init__(self, nc):
        self.nc = nc
        self.streams = {e: [] for e in ENGS}
        self.ev_recs = {vk: [] for vk in VKS}
        self.priv_vks = []

    def op(self, eng, fn, r=(), w=(), dma=False, dma_vk=None):
        deps = {}
        rr = _norm(r)
        ww = _norm(w)
        for b, lo, hi in rr:
            for ev in b.rdeps(lo, hi):
                if deps.get(ev.vk, 0) < ev.n:
                    deps[ev.vk] = ev.n
        for b, lo, hi in ww:
            for ev in b.wdeps(lo, hi):
                if deps.get(ev.vk, 0) < ev.n:
                    deps[ev.vk] = ev.n
        vk = dma_vk if dma_vk else (DMAQ[eng] if dma else eng)
        if dma and not vk.startswith("pv"):
            prev = len(self.ev_recs[vk])
            if prev and deps.get(vk, 0) < prev:
                deps[vk] = prev
        ev = Ev(vk, len(self.ev_recs[vk]) + 1)
        waits = [Ev(k, n) for k, n in deps.items()]
        rec = Rec(fn, waits, ev)
        self.streams[eng].append(rec)
        self.ev_recs[vk].append(rec)
        for b, lo, hi in rr:
            b.note_read(lo, hi, ev)
        for b, lo, hi in ww:
            b.note_write(lo, hi, ev)
        return ev

    def dma(self, eng, dst_ap, src_ap, r=(), w=(), **kw):
        return self.op(eng, lambda e: e.dma_start(dst_ap, src_ap, **kw),
                       r=r, w=w, dma=True)

    def dma_priv(self, eng, dst_ap, src_ap, r=(), w=(), **kw):
        vk = "pv%d" % len(self.priv_vks)
        self.priv_vks.append(vk)
        self.ev_recs[vk] = []
        return self.op(eng, lambda e: e.dma_start(dst_ap, src_ap, **kw),
                       r=r, w=w, dma=True, dma_vk=vk)

    def emit(self):
        nc = self.nc
        all_vks = list(self.ev_recs.keys())

        def is_dma(vk):
            return vk.startswith("d_") or vk.startswith("pv")
        for vk in all_vks:
            if is_dma(vk):
                for rec in self.ev_recs[vk]:
                    rec.marked = True
        for stream in self.streams.values():
            for rec in stream:
                for ev in rec.waits:
                    self.ev_recs[ev.vk][ev.n - 1].marked = True
        pref = {}
        for vk in all_vks:
            c, lst = 0, [0]
            for rec in self.ev_recs[vk]:
                if rec.marked:
                    c += 1
                lst.append(c)
            pref[vk] = lst
        sems = {vk: nc.alloc_semaphore(name="sem_" + vk) for vk in all_vks}

        def section(engobj, eng):
            hw = {vk: 0 for vk in all_vks}
            for rec in self.streams[eng]:
                for ev in sorted(rec.waits, key=lambda e: e.vk):
                    val = pref[ev.vk][ev.n]
                    if val > hw[ev.vk]:
                        mult = 16 if is_dma(ev.vk) else 1
                        engobj.wait_ge(sems[ev.vk], val * mult)
                        hw[ev.vk] = val
                inst = rec.fn(engobj)
                if rec.marked:
                    if isinstance(inst, (list, tuple)):
                        inst = inst[-1]
                    mult = 16 if is_dma(rec.ev.vk) else 1
                    inst.then_inc(sems[rec.ev.vk], mult)

        with nc.Block() as block:
            block.tensor(lambda e: section(e, "pe"))
            block.scalar(lambda e: section(e, "act"))
            block.vector(lambda e: section(e, "dve"))
            block.gpsimd(lambda e: section(e, "pool"))
            block.sync(lambda e: section(e, "sp"))


class Arena:

    def __init__(self, nc, base):
        self.nc = nc
        self.base = base
        self.cur = base
        self.buf = Buf("arena")
        self.n = 0

    def alloc(self, shape, dtype, align=64):
        nbytes = int(np.prod(shape[1:])) * mybir.dt.size(dtype)
        off = (self.cur + align - 1) // align * align
        assert off + nbytes <= SBUF_TOP, (
            f"arena overflow: need {nbytes} at {off}, top {SBUF_TOP}")
        self.cur = off + nbytes
        self.n += 1
        h = self.nc.alloc_sbuf_tensor_at(f"ar{self.n}", list(shape), dtype,
                                         offset=off, align_bytes=align)
        return h, (self.buf, off, off + nbytes)

    def reset(self):
        self.cur = self.base



class Ctx:

    def __init__(self, nc):
        self.nc = nc
        self.p = Prog(nc)
        self.pb = []
        self.pb_buf = []
        self.pb_i = 0
        for i in range(8):
            h = nc.alloc_psum_tensor(f"pb{i}", [128, 512], F32)
            self.pb.append(h)
            self.pb_buf.append(Buf(f"pb{i}"))
        self.ident = nc.alloc_sbuf_tensor("ident", [128, 128], BFR16)
        self.identh = nc.alloc_sbuf_tensor("identh", [128, 128], BF16)
        self.ones_row = nc.alloc_sbuf_tensor("ones_row", [1, 512], BF16)
        self.stat = nc.alloc_sbuf_tensor("stat", [128, 64], F32)
        self.stat_buf = Buf("stat")
        self.sq = nc.alloc_sbuf_tensor("sq", [128, 1024], F32)
        self.sq_buf = Buf("sq")
        self.const_buf = Buf("const")
        self.cft = nc.alloc_sbuf_tensor("cft", [128, 16], F32)
        self._constf = {}
        self.ident_dr = nc.inline_tensor(np.eye(128, dtype=BFR), name="ident_c")
        self.identh_dr = nc.inline_tensor(np.eye(128, dtype=BF), name="identh_c")
        ones_np = np.ones((1, 512), dtype=BF)
        self.ones_dr = nc.inline_tensor(ones_np, name="ones_c")
        self.xt = [nc.alloc_sbuf_tensor(f"xt{i}", [128, 1024], F32) for i in range(4)]
        self.xt_buf = [Buf(f"xt{i}") for i in range(4)]
        self.xn = [nc.alloc_sbuf_tensor(f"xn{i}", [128, 1024], BF16) for i in range(4)]
        self.xn_buf = [Buf(f"xn{i}") for i in range(4)]
        base = nc.sbuf_base
        base = (base + 63) // 64 * 64
        self.arena = Arena(nc, base)
        self.p.dma_priv("sp", self.ident[:], self.ident_dr.ap(), w=[self.const_buf])
        self.p.dma_priv("sp", self.identh[:], self.identh_dr.ap(), w=[self.const_buf])
        self.p.dma_priv("sp", self.ones_row[:], self.ones_dr.ap(), w=[self.const_buf])

    def idt(self, ap):
        return self.ident if ap.dtype == BFR16 else self.identh

    def bank(self):
        i = self.pb_i % 8
        self.pb_i += 1
        return self.pb[i], self.pb_buf[i]

    def constf(self, val):
        if val not in self._constf:
            i = len(self._constf)
            assert i < 16
            ap = self.cft[:, i:i + 1]
            self.p.op("pool", lambda e, ap=ap, v=val: e.memset(ap, v),
                      w=[self.const_buf])
            self._constf[val] = ap
        return self._constf[val]

    def stat_slot(self, i):
        return self.stat[:, i:i + 1], (self.stat_buf, i, i + 1)



def ln_tile(c, xt, xt_dep, xn_out, xn_w, s0, eps=1e-5, d=D):
    p = c.p
    sum_ap, sum_k = c.stat_slot(s0)
    nm_ap, nm_k = c.stat_slot(s0 + 1)
    var_ap, var_k = c.stat_slot(s0 + 2)
    rs_ap, rs_k = c.stat_slot(s0 + 3)
    p.op("dve", lambda e: e.reduce_sum(sum_ap, xt[:, 0:d], axis=AX.X),
         r=[xt_dep], w=[sum_k])
    p.op("dve", lambda e: e.tensor_scalar_mul(nm_ap, sum_ap, -1.0 / d),
         r=[sum_k], w=[nm_k])
    p.op("act", lambda e: e.activation(c.sq[:, 0:d], xt[:, 0:d], AF.Square,
                                       bias=nm_ap, accum_out=var_ap),
         r=[xt_dep, nm_k], w=[c.sq_buf, var_k])
    eps_ap = c.constf(eps)
    p.op("act", lambda e: e.activation(rs_ap, var_ap, AF.Sqrt,
                                       scale=1.0 / d, bias=eps_ap),
         r=[var_k, c.const_buf], w=[rs_k])
    p.op("dve", lambda e: e.reciprocal(rs_ap, rs_ap), r=[rs_k], w=[rs_k])
    ev = p.op("dve", lambda e: e.tensor_scalar(xn_out[:, 0:d], xt[:, 0:d],
                                               nm_ap, rs_ap, ALU.add, ALU.mult),
              r=[xt_dep, nm_k, rs_k], w=[xn_w])
    return ev


def transpose_128(c, dst_ap, dst_w, src_ap, src_deps, n_in=8, pack_bank=None):
    p = c.p
    bank, bank_buf = pack_bank if pack_bank else c.bank()
    bb = bank.bitcast(src_ap.dtype)
    idn = c.idt(src_ap)

    def f(e):
        last = None
        for j in range(n_in):
            last = e.transpose(bb[:, j * 128:(j + 1) * 128],
                               src_ap[:, j * 128:(j + 1) * 128], idn[:])
        return last
    ev = p.op("pe", f, r=list(src_deps) + [c.const_buf], w=[(bank_buf, 0, 1 << 30)])
    src2 = bb[:, 0:n_in * 128]
    if dst_ap.ndim == 3:
        src2 = src2.rearrange("p (j q) -> p j q", q=128)
    ev2 = p.op("dve", lambda e: e.tensor_copy(dst_ap, src2),
               r=[(bank_buf, 0, 1 << 30)], w=[dst_w])
    return ev2


def mm(c, psum_pair, lhsT_rhs_list, r_deps, n_lo=0, n_hi=None, m_out=128):
    p = c.p
    bank, bank_buf = psum_pair
    if n_hi is None:
        n_hi = n_lo + lhsT_rhs_list[0][1].shape[-1]

    def f(e):
        last = None
        n = len(lhsT_rhs_list)
        for i, (lt, rh) in enumerate(lhsT_rhs_list):
            last = e.matmul(bank[0:m_out, n_lo:n_hi], lt, rh,
                            start=(i == 0), stop=(i == n - 1))
        return last
    return p.op("pe", f, r=list(r_deps), w=[(bank_buf, 0, 1 << 30)])



def load_weight(c, dram_ap, rows, cols, name, cast=True):
    p = c.p
    nr = rows // 128
    h, key = c.arena.alloc([128, nr * cols], BF16)
    src = dram_ap.rearrange("(r p) c -> r p c", p=128)
    evs = []
    for r in range(nr):
        evs.append(p.dma("pool", h[:, r * cols:(r + 1) * cols], src[r], w=[key]))
    return h, key, evs


def load_weight_T(c, dram_ap, rows, cols, tag, raw=None):
    p = c.p
    nr = rows // 128
    ncc = cols // 128
    wT, wT_key = c.arena.alloc([128, ncc * rows], BF16)
    if raw is None:
        raw = [c.arena.alloc([128, cols], BF16) for _ in range(2)]
    raw = [(h if isinstance(h, bass.AP) else h[:], k) for h, k in raw]
    src = dram_ap.rearrange("(r p) c -> r p c", p=128)
    for r in range(nr):
        rh, rk = raw[r % 2]
        lev = p.dma("pool", rh[:, 0:cols], src[r], w=[rk])
        for j0 in range(0, ncc, 8):
            jn = min(8, ncc - j0)
            bank, bank_buf = c.bank()
            bb = bank.bitcast(BF16)

            def f(e, rh=rh, j0=j0, jn=jn, bb=bb):
                last = None
                for j in range(jn):
                    last = e.transpose(bb[:, j * 128:(j + 1) * 128],
                                       rh[:, (j0 + j) * 128:(j0 + j + 1) * 128],
                                       c.identh[:])
                return last
            pev = p.op("pe", f, r=[rk, c.const_buf], w=[(bank_buf, 0, 1 << 30)])
            dst = wT[:].rearrange("p (j q) -> p j q", q=rows)[:, j0:j0 + jn,
                                                             r * 128:(r + 1) * 128]
            src2 = bb[:, 0:jn * 128].rearrange("p (j q) -> p j q", q=128)
            p.op("dve", lambda e, dst=dst, src2=src2:
                 e.tensor_copy(dst, src2),
                 r=[(bank_buf, 0, 1 << 30)], w=[wT_key])
    return wT, wT_key



def seg_l1_ffn(c, xin, xin_buf, xout, xout_buf, W):
    p = c.p
    c.arena.reset()
    GT = 256
    NG = T // GT
    raw = [c.arena.alloc([128, 3072], BF16) for _ in range(2)]
    wgT, wgT_key = load_weight_T(c, W["L1_ffn_w_gate"], 3072, 1024, "wg",
                                 raw=[(h[:, 0:1024], k) for h, k in raw])
    wuT, wuT_key = load_weight_T(c, W["L1_ffn_w_up"], 3072, 1024, "wu",
                                 raw=[(h[:, 0:1024], k) for h, k in raw])
    wdT, wdT_key = load_weight_T(c, W["L1_ffn_w_down"], 1024, 3072, "wd",
                                 raw=raw)
    NH = 3072 // 128
    xnt_g, xnt_g_key = c.arena.alloc([128, KD * GT], BF16)
    ht, ht_key = c.arena.alloc([128, NH * GT], BF16)
    xin_t = xin.rearrange("(t p) c -> t p c", p=128)
    xout_t = xout.rearrange("(t p) c -> t p c", p=128)
    s = 1.0 / 127.0

    for g in range(NG):
        t0 = g * (GT // 128)
        nt = GT // 128
        xn_evs = []
        xt_evs = {}
        for ti in range(nt):
            t = t0 + ti
            ring = t % 4
            xt_h, xt_b = c.xt[ring], c.xt_buf[ring]
            lev = p.dma("sp", xt_h[:], xin_t[t], r=[(xin_buf, t, t + 1)], w=[xt_b])
            xt_evs[ti] = (xt_h, xt_b)
            xn_h, xn_b = c.xn[ring], c.xn_buf[ring]
            ln_tile(c, xt_h, xt_b, xn_h, xn_b, s0=4 * ring)
            dst = xnt_g[:].rearrange("p (j q) -> p j q", q=GT)[:, :,
                                                              ti * 128:(ti + 1) * 128]
            transpose_128(c, dst, xnt_g_key, xn_h[:], [xn_b])
        xnt_dep = xnt_g_key
        ht_dep_all = ht_key
        for m in range(NH):
            bank_g = c.bank()
            mm(c, bank_g,
               [(wgT[:, j * 3072 + m * 128: j * 3072 + (m + 1) * 128],
                 xnt_g[:, j * GT:(j + 1) * GT]) for j in range(KD)],
               [wgT_key, xnt_dep])
            ht_slice = ht[:, m * GT:(m + 1) * GT]
            ht_k = (c.arena.buf, ht_key[1] + m * GT * 2, ht_key[1] + (m + 1) * GT * 2)
            p.op("act", lambda e, b=bank_g[0], hs=ht_slice:
                 e.activation(hs, b[:, 0:GT], AF.Silu, scale=s),
                 r=[(bank_g[1], 0, 4 * GT)], w=[ht_k])
            bank_u = c.bank()
            mm(c, bank_u,
               [(wuT[:, j * 3072 + m * 128: j * 3072 + (m + 1) * 128],
                 xnt_g[:, j * GT:(j + 1) * GT]) for j in range(KD)],
               [wuT_key, xnt_dep])
            p.op("dve", lambda e, b=bank_u[0], hs=ht_slice:
                 e.scalar_tensor_tensor(hs, b[:, 0:GT], s, hs, ALU.mult, ALU.mult),
                 r=[(bank_u[1], 0, 4 * GT), ht_k], w=[ht_k])
        for ti in range(nt):
            t = t0 + ti
            xt_h, xt_b = xt_evs[ti]
            for half in range(2):
                bank_d = c.bank()
                mm(c, bank_d,
                   [(ht[:, k * GT + ti * 128: k * GT + (ti + 1) * 128],
                     wdT[:, k * 1024 + half * 512: k * 1024 + (half + 1) * 512])
                    for k in range(NH)],
                   [ht_dep_all, wdT_key])
                p.op("dve", lambda e, b=bank_d[0], xt=xt_h, half=half:
                     e.scalar_tensor_tensor(xt[:, half * 512:(half + 1) * 512],
                                            b[:], s,
                                            xt[:, half * 512:(half + 1) * 512],
                                            ALU.mult, ALU.add),
                     r=[(bank_d[1], 0, 2048), xt_b], w=[xt_b])
            p.dma("act", xout_t[t], xt_h[:], r=[xt_b], w=[(xout_buf, t, t + 1)])


def stage_row(c, dram_ap, n, cast=True):
    h, key = c.arena.alloc([1, n], BF16)
    src = dram_ap
    if src.ndim == 1:
        src = src.unsqueeze(0)
    c.p.dma_priv("pool", h[:], src, w=[key])
    return h[:], key


def stage_col(c, dram_ap_flat, n, tag=""):
    nj = n // 128
    h, key = c.arena.alloc([128, nj], F32)
    src = dram_ap_flat.rearrange("(j p) -> p j", p=128)
    c.p.dma_priv("pool", h[:], src, w=[key], allow_slow_non_contiguous=True)
    return h, key


def seg_l3_ffn(c, xin, xin_buf, xout, xout_buf, W):
    p = c.p
    c.arena.reset()
    E, DFF = 8, 4096
    GT = 256
    NG = T // GT
    ntg = GT // 128
    NK2 = DFF // 128
    wr, wr_key = c.arena.alloc([128, KD * 8], BF16)
    p.dma_priv("pool", wr[:].rearrange("p (k n) -> p k n", n=8),
               W["L3_ffn_Wr"].rearrange("(k p) n -> p k n", p=128), w=[wr_key])
    br, br_key = stage_row(c, W["L3_ffn_br"], 8)
    b1r, b1r_key = c.arena.alloc([1, DFF], BF16)
    b2r, b2r_key = c.arena.alloc([1, E * 1024], BF16)
    p.dma_priv("pool", b2r[:].rearrange("o (e n) -> o e n", n=1024),
               W["L3_ffn_b2"], w=[b2r_key])
    w1sb, w1_key = c.arena.alloc([128, KD * DFF], BF16)
    w2sb, w2_key = c.arena.alloc([128, NK2 * 1024], BF16)
    xnt_g, xnt_key = c.arena.alloc([128, KD * GT], BF16)
    h1t, h1t_key = c.arena.alloc([128, NK2 * GT], BF16)
    gm, gm_key = c.arena.alloc([128, E * ntg], F32)
    lsb, lsb_key = c.arena.alloc([128, 16 * ntg], F32)
    xin_t = xin.rearrange("(t p) c -> t p c", p=128)
    xout_t = xout.rearrange("(t p) c -> t p c", p=128)
    w1_dr = W["L3_ffn_W1"]
    w2_dr = W["L3_ffn_W2"]

    for g in range(NG):
        t0 = g * ntg
        xt_of = {}
        for ti in range(ntg):
            t = t0 + ti
            ring = t % 4
            xt_h, xt_b = c.xt[ring], c.xt_buf[ring]
            p.dma("sp", xt_h[:], xin_t[t], r=[(xin_buf, t, t + 1)], w=[xt_b])
            xt_of[ti] = (xt_h, xt_b)
            xn_h, xn_b = c.xn[ring], c.xn_buf[ring]
            ln_tile(c, xt_h, xt_b, xn_h, xn_b, s0=4 * ring)
            dst = xnt_g[:].rearrange("p (j q) -> p j q", q=GT)[:, :, ti * 128:(ti + 1) * 128]
            transpose_128(c, dst, xnt_key, xn_h[:], [xn_b])
            bank_r = c.bank()
            mm(c, bank_r,
               [(xnt_g[:].rearrange("p (j q) -> p j q", q=GT)[:, j, ti * 128:(ti + 1) * 128],
                 wr[:, j * 8:(j + 1) * 8]) for j in range(KD)]
               + [(c.ones_row[0:1, 0:128], br)],
               [xnt_key, wr_key, br_key, c.const_buf])
            lo = ti * 16
            lg = (c.arena.buf, lsb_key[1] + lo * 4, lsb_key[1] + (lo + 16) * 4)
            p.op("dve", lambda e, b=bank_r[0], lo=lo: e.tensor_copy(
                lsb[:, lo:lo + 8], b[:, 0:8]), r=[(bank_r[1], 0, 64)], w=[lg])
            p.op("dve", lambda e, lo=lo: e.reduce_max(
                lsb[:, lo + 8:lo + 9], lsb[:, lo:lo + 8], axis=AX.X, negate=True),
                r=[lg], w=[lg])
            p.op("act", lambda e, lo=lo: e.activation(
                c.sq[:, 0:8], lsb[:, lo:lo + 8], AF.Exp,
                bias=lsb[:, lo + 8:lo + 9], accum_out=lsb[:, lo + 9:lo + 10]),
                r=[lg], w=[c.sq_buf, lg])
            p.op("dve", lambda e, lo=lo: e.reciprocal(
                lsb[:, lo + 11:lo + 12], lsb[:, lo + 9:lo + 10]), r=[lg], w=[lg])
            p.op("dve", lambda e, lo=lo: e.tensor_scalar_mul(
                lsb[:, lo + 8:lo + 9], lsb[:, lo + 8:lo + 9], -1.0), r=[lg], w=[lg])
            for ex in range(E):
                p.op("dve", lambda e, lo=lo, ex=ex, ti=ti: e.tensor_scalar(
                    gm[:, ex * ntg + ti:ex * ntg + ti + 1],
                    lsb[:, lo + ex:lo + ex + 1],
                    lsb[:, lo + 8:lo + 9], lsb[:, lo + 11:lo + 12],
                    ALU.is_equal, ALU.mult),
                    r=[lg], w=[(c.arena.buf, gm_key[1] + (ex * ntg + ti) * 4,
                                gm_key[1] + (ex * ntg + ti + 1) * 4)])
        xnt_dep = xnt_key
        for ex in range(E):
            p.dma("pool", b1r[:], W["L3_ffn_b1"][ex].unsqueeze(0), w=[b1r_key])
            p.dma("pool", w1sb[:].rearrange("p (k n) -> p k n", n=DFF),
                  w1_dr[ex].rearrange("(k p) n -> p k n", p=128), w=[w1_key])
            p.dma("pool", w2sb[:].rearrange("p (k n) -> p k n", n=1024),
                  w2_dr[ex].rearrange("(k p) n -> p k n", p=128), w=[w2_key])
            for m in range(NK2):
                bank = c.bank()
                mm(c, bank,
                   [(w1sb[:].rearrange("p (j n) -> p j n", n=DFF)[:, j, m * 128:(m + 1) * 128],
                     xnt_g[:, j * GT:(j + 1) * GT]) for j in range(KD)]
                   + [(b1r[0:1, m * 128:(m + 1) * 128], c.ones_row[0:1, 0:GT])],
                   [w1_key, xnt_dep, b1r_key, c.const_buf])
                hk = (c.arena.buf, h1t_key[1] + m * GT * 2, h1t_key[1] + (m + 1) * GT * 2)
                p.op("act", lambda e, b=bank[0], m=m: e.activation(
                    h1t[:, m * GT:(m + 1) * GT], b[:, 0:GT], AF.Relu),
                    r=[(bank[1], 0, 4 * GT)], w=[hk])
            for ti in range(ntg):
                xt_h, xt_b = xt_of[ti]
                for half in range(2):
                    bank = c.bank()
                    mm(c, bank,
                       [(h1t[:, k * GT + ti * 128: k * GT + (ti + 1) * 128],
                         w2sb[:, k * 1024 + half * 512: k * 1024 + (half + 1) * 512])
                        for k in range(NK2)]
                       + [(c.ones_row[0:1, 0:128],
                           b2r[0:1, ex * 1024 + half * 512: ex * 1024 + (half + 1) * 512])],
                       [h1t_key, w2_key, b2r_key, c.const_buf])
                    gmk = (c.arena.buf, gm_key[1] + (ex * ntg + ti) * 4,
                           gm_key[1] + (ex * ntg + ti + 1) * 4)
                    p.op("dve", lambda e, b=bank[0], xt=xt_h, half=half, ex=ex, ti=ti:
                         e.scalar_tensor_tensor(
                             xt[:, half * 512:(half + 1) * 512], b[:],
                             gm[:, ex * ntg + ti:ex * ntg + ti + 1],
                             xt[:, half * 512:(half + 1) * 512],
                             ALU.mult, ALU.add),
                         r=[(bank[1], 0, 2048), gmk, xt_b], w=[xt_b])
        for ti in range(ntg):
            t = t0 + ti
            xt_h, xt_b = xt_of[ti]
            p.dma("act", xout_t[t], xt_h[:], r=[xt_b], w=[(xout_buf, t, t + 1)])


def seg_l0_attn(c, xin, xin_buf, xout, xout_buf, W):
    p = c.p
    nc = c.nc
    c.arena.reset()
    wqT, wqT_key = load_weight_T(c, W["L0_attn_wq"], 1024, 1024, "wq")
    wuT, wuT_key = load_weight_T(c, W["L0_attn_wu"], 1024, 1024, "wu")
    bq, bq_key = stage_row(c, W["L0_attn_bq"], 1024)
    bu, bu_key = stage_row(c, W["L0_attn_bu"], 1024)
    qd = nc.dram_tensor("qd_scr", [T, 1024], BFR16, kind="Internal").ap()
    od = nc.dram_tensor("od_scr", [T, 1024], BF16, kind="Internal").ap()
    qd_buf, od_buf = Buf("qd"), Buf("od")
    qsb, qsb_key = c.arena.alloc([128, 1024], BFR16)
    mq, mq_key = c.arena.alloc([128, 1024], BFR16)
    mt, mt_key = c.arena.alloc([64, 2048], BFR16)
    esb, esb_key = c.arena.alloc([128, 16 * 2048], BFR16)
    osb, osb_key = c.arena.alloc([128, 1024], BF16)
    osb2, osb2_key = c.arena.alloc([128, 1024], BF16)
    ot, ot_key = c.arena.alloc([128, 1024], BF16)
    es, es_key = c.arena.alloc([128, 96], F32)
    xin_t = xin.rearrange("(t p) c -> t p c", p=128)
    xout_t = xout.rearrange("(t p) c -> t p c", p=128)
    qd_t = qd.rearrange("(t p) c -> t p c", p=128)
    od_t = od.rearrange("(t p) c -> t p c", p=128)

    for t in range(NT):
        ring = t % 4
        xt_h, xt_b = c.xt[ring], c.xt_buf[ring]
        p.dma("sp", xt_h[:], xin_t[t], r=[(xin_buf, t, t + 1)], w=[xt_b])
        xn_h, xn_b = c.xn[ring], c.xn_buf[ring]
        ln_tile(c, xt_h, xt_b, xn_h, xn_b, s0=4 * ring)
        xnt, xnt_key = c.arena.alloc([128, 1024], BF16) if t == 0 else (c._l0_xnt, c._l0_xnt_key)
        if t == 0:
            c._l0_xnt, c._l0_xnt_key = xnt, xnt_key
        dst = xnt[:].rearrange("p (j q) -> p j q", q=128)
        transpose_128(c, dst, xnt_key, xn_h[:], [xn_b])
        for half in range(2):
            bank = c.bank()
            mm(c, bank,
               [(xnt[:, j * 128:(j + 1) * 128],
                 wqT[:, j * 1024 + half * 512: j * 1024 + (half + 1) * 512])
                for j in range(KD)]
               + [(c.ones_row[0:1, 0:128], bq[0:1, half * 512:(half + 1) * 512])],
               [xnt_key, wqT_key, bq_key, c.const_buf])
            p.op("dve", lambda e, b=bank[0], half=half: e.tensor_copy(
                qsb[:, half * 512:(half + 1) * 512], b[:]),
                r=[(bank[1], 0, 2048)], w=[qsb_key])
        p.dma("act", qd_t[t], qsb[:], r=[qsb_key], w=[(qd_buf, t, t + 1)])
        qd_flat = qd.rearrange("t c -> (t c)")[t * 131072:(t + 1) * 131072]
        p.dma("sp", mq[:].rearrange("p (j d) -> p j d", d=64),
              qd_flat.rearrange("(j p d) -> p j d", j=16, p=128),
              r=[(qd_buf, t, t + 1)], w=[mq_key])
        for g in range(2):
            bank, bank_buf = c.bank()
            bb = bank.bitcast(BFR16)

            def f(e, g=g, bb=bb):
                last = None
                for j in range(8):
                    last = e.transpose(bb[0:64, j * 128:(j + 1) * 128],
                                       mq[:, (g * 8 + j) * 64:(g * 8 + j + 1) * 64],
                                       c.ident[:])
                return last
            p.op("pe", f, r=[mq_key, c.const_buf], w=[(bank_buf, 0, 1 << 30)])
            p.op("dve", lambda e, g=g, bb=bb: e.tensor_copy(
                mt[0:64, g * 1024:(g + 1) * 1024], bb[0:64, 0:1024]),
                r=[(bank_buf, 0, 1 << 30)], w=[mt_key])
        for a in range(16):
            for q4 in range(4):
                bank = c.bank()
                mm(c, bank, [(mt[0:64, a * 128:(a + 1) * 128],
                              mt[0:64, q4 * 512:(q4 + 1) * 512])],
                   [mt_key])
                p.op("act", lambda e, b=bank[0], a=a, q4=q4: e.activation(
                    esb[:, a * 2048 + q4 * 512: a * 2048 + (q4 + 1) * 512],
                    b[:], AF.Exp, scale=1.0 / 32.0,
                    accum_out=es[:, a * 4 + q4:a * 4 + q4 + 1]),
                    r=[(bank[1], 0, 2048)], w=[esb_key, (c.arena.buf,
                        es_key[1] + (a * 4 + q4) * 4, es_key[1] + (a * 4 + q4 + 1) * 4)])
            p.op("dve", lambda e, a=a: e.reduce_sum(
                es[:, 64 + a:64 + a + 1], es[:, a * 4:(a + 1) * 4], axis=AX.X),
                r=[(c.arena.buf, es_key[1] + a * 16, es_key[1] + (a + 1) * 16)],
                w=[(c.arena.buf, es_key[1] + (64 + a) * 4, es_key[1] + (64 + a + 1) * 4)])
        p.op("dve", lambda e: e.reciprocal(es[:, 80:96], es[:, 64:80]),
             r=[(c.arena.buf, es_key[1] + 256, es_key[1] + 320)],
             w=[(c.arena.buf, es_key[1] + 320, es_key[1] + 384)])
        for b in range(16):
            bank = c.bank()
            mm(c, bank,
               [(esb[:, a * 2048 + b * 128: a * 2048 + (b + 1) * 128],
                 mq[:, a * 64:(a + 1) * 64]) for a in range(16)],
               [esb_key, mq_key])
            p.op("dve", lambda e, bk=bank[0], b=b: e.tensor_scalar_mul(
                osb[:, b * 64:(b + 1) * 64], bk[:, 0:64], es[:, 80 + b:80 + b + 1]),
                r=[(bank[1], 0, 256), (c.arena.buf, es_key[1] + 320, es_key[1] + 384)],
                w=[osb_key])
        od_flat = od.rearrange("t c -> (t c)")[t * 131072:(t + 1) * 131072]
        p.dma("act", od_flat.rearrange("(b p d) -> p b d", b=16, p=128),
              osb[:].rearrange("p (b d) -> p b d", d=64),
              r=[osb_key], w=[(od_buf, t, t + 1)])
        p.dma("sp", osb2[:], od_t[t], r=[(od_buf, t, t + 1)], w=[osb2_key])
        dst2 = ot[:].rearrange("p (j q) -> p j q", q=128)
        transpose_128(c, dst2, ot_key, osb2[:], [osb2_key])
        for half in range(2):
            bank = c.bank()
            mm(c, bank,
               [(ot[:, j * 128:(j + 1) * 128],
                 wuT[:, j * 1024 + half * 512: j * 1024 + (half + 1) * 512])
                for j in range(KD)]
               + [(c.ones_row[0:1, 0:128], bu[0:1, half * 512:(half + 1) * 512])],
               [ot_key, wuT_key, bu_key, c.const_buf])
            p.op("dve", lambda e, b=bank[0], xt=xt_h, half=half:
                 e.tensor_tensor(xt[:, half * 512:(half + 1) * 512], b[:],
                                 xt[:, half * 512:(half + 1) * 512], ALU.add),
                 r=[(bank[1], 0, 2048), xt_b], w=[xt_b])
        p.dma("act", xout_t[t], xt_h[:], r=[xt_b], w=[(xout_buf, t, t + 1)])


def seg_l3_attn(c, xin, xin_buf, xout, xout_buf, W):
    p = c.p
    c.arena.reset()
    wbRaw, wbRaw_key = c.arena.alloc([64, 1024], BF16)
    p.dma_priv("pool", wbRaw[:], W["L3_attn_W_basis"], w=[wbRaw_key])
    wbT, wbT_key = c.arena.alloc([128, KD * 64], BF16)
    for g in range(1):
        bank, bank_buf = c.bank()
        bb = bank.bitcast(BF16)

        def f(e, bb=bb):
            last = None
            for j in range(8):
                last = e.transpose(bb[:, j * 64:(j + 1) * 64],
                                   wbRaw[0:64, j * 128:(j + 1) * 128],
                                   c.identh[0:64, 0:64])
            return last
        p.op("pe", f, r=[wbRaw_key, c.const_buf], w=[(bank_buf, 0, 1 << 30)])
        p.op("dve", lambda e, bb=bb: e.tensor_copy(wbT[:], bb[:, 0:KD * 64]),
             r=[(bank_buf, 0, 1 << 30)], w=[wbT_key])
    wvT, wvT_key = load_weight_T(c, W["L3_attn_W_v"], 1024, 1024, "wv")
    woT, woT_key = load_weight_T(c, W["L3_attn_W_o"], 1024, 1024, "wo")
    bv, bv_key = stage_row(c, W["L3_attn_b_v"], 1024)
    bo, bo_key = stage_row(c, W["L3_attn_b_o"], 1024)
    tril01_dr = c.nc.inline_tensor(np.tril(np.ones((128, 128), dtype=BF)), name="tril01")
    triu01_dr = c.nc.inline_tensor(np.triu(np.ones((128, 128), dtype=BF)), name="triu01")
    tril01, tril_key = c.arena.alloc([128, 128], BF16)
    triu01, triu_key = c.arena.alloc([128, 128], BF16)
    p.dma_priv("sp", tril01[:], tril01_dr.ap(), w=[tril_key])
    p.dma_priv("sp", triu01[:], triu01_dr.ap(), w=[triu_key])

    vsb, vsb_key = c.arena.alloc([128, NTS * 1024], BFR16)
    latT, latT_key = c.arena.alloc([64, S], BF16)
    esb, esb_key = c.arena.alloc([128, NTS * S], BFR16)
    edmL, edmL_key = c.arena.alloc([128, NTS * 128], BFR16)
    edmU, edmU_key = c.arena.alloc([128, NTS * 128], BFR16)
    ysb, ysb_key = c.arena.alloc([128, 1024], BF16)
    yt, yt_key = c.arena.alloc([128, 1024], BF16)
    es, es_key = c.arena.alloc([128, 64], F32)
    xnt, xnt_key = c.arena.alloc([128, 1024], BF16)
    xin_t = xin.rearrange("(t p) c -> t p c", p=128)
    xout_t = xout.rearrange("(t p) c -> t p c", p=128)

    def esk(lo, hi):
        return (c.arena.buf, es_key[1] + lo * 4, es_key[1] + hi * 4)

    for sq in range(BPC):
        base = sq * NTS
        for a in range(NTS):
            t = base + a
            ring = t % 4
            xt_h, xt_b = c.xt[ring], c.xt_buf[ring]
            p.dma("sp", xt_h[:], xin_t[t], r=[(xin_buf, t, t + 1)], w=[xt_b])
            xn_h, xn_b = c.xn[ring], c.xn_buf[ring]
            ln_tile(c, xt_h, xt_b, xn_h, xn_b, s0=4 * ring)
            dst = xnt[:].rearrange("p (j q) -> p j q", q=128)
            transpose_128(c, dst, xnt_key, xn_h[:], [xn_b])
            bank = c.bank()
            mm(c, bank, [(xnt[:, j * 128:(j + 1) * 128], wbT[:, j * 64:(j + 1) * 64])
                         for j in range(KD)], [xnt_key, wbT_key])
            lat_tmp, lat_tmp_key = (c.sq, c.sq_buf)
            p.op("dve", lambda e, b=bank[0]: e.tensor_copy(ysb[:, 0:64], b[:, 0:64]),
                 r=[(bank[1], 0, 256)], w=[ysb_key])
            bank2, bank2_buf = c.bank()
            bb2 = bank2.bitcast(BF16)
            p.op("pe", lambda e, bb2=bb2: e.transpose(
                bb2[0:64, 0:128], ysb[:, 0:64], c.identh[:]),
                r=[ysb_key, c.const_buf], w=[(bank2_buf, 0, 1 << 30)])
            p.op("dve", lambda e, bb2=bb2, a=a: e.tensor_copy(
                latT[0:64, a * 128:(a + 1) * 128], bb2[0:64, 0:128]),
                r=[(bank2_buf, 0, 1 << 30)], w=[latT_key])
            for half in range(2):
                bank3 = c.bank()
                mm(c, bank3,
                   [(xnt[:, j * 128:(j + 1) * 128],
                     wvT[:, j * 1024 + half * 512: j * 1024 + (half + 1) * 512])
                    for j in range(KD)]
                   + [(c.ones_row[0:1, 0:128], bv[0:1, half * 512:(half + 1) * 512])],
                   [xnt_key, wvT_key, bv_key, c.const_buf])
                p.op("dve", lambda e, b=bank3[0], a=a, half=half: e.tensor_copy(
                    vsb[:, a * 1024 + half * 512: a * 1024 + (half + 1) * 512], b[:]),
                    r=[(bank3[1], 0, 2048)], w=[vsb_key])
        for a in range(NTS):
            for c4 in range(4):
                bank = c.bank()
                mm(c, bank, [(latT[0:64, a * 128:(a + 1) * 128],
                              latT[0:64, c4 * 512:(c4 + 1) * 512])], [latT_key])
                p.op("act", lambda e, b=bank[0], a=a, c4=c4: e.activation(
                    esb[:, a * S + c4 * 512: a * S + (c4 + 1) * 512],
                    b[:], AF.Exp, scale=0.125),
                    r=[(bank[1], 0, 2048)], w=[esb_key])
            dslice = esb[:, a * S + a * 128: a * S + (a + 1) * 128]
            p.op("dve", lambda e, a=a, dslice=dslice: e.scalar_tensor_tensor(
                edmL[:, a * 128:(a + 1) * 128], dslice, 1.0, tril01[:],
                ALU.mult, ALU.mult, accum_out=es[:, a:a + 1]),
                r=[esb_key, tril_key], w=[edmL_key, esk(a, a + 1)])
            p.op("dve", lambda e, a=a, dslice=dslice: e.scalar_tensor_tensor(
                edmU[:, a * 128:(a + 1) * 128], dslice, 1.0, triu01[:],
                ALU.mult, ALU.mult),
                r=[esb_key, triu_key], w=[edmU_key])
            if a > 0:
                p.op("dve", lambda e, a=a: e.reduce_sum(
                    es[:, 16 + a:16 + a + 1], esb[:, a * S: a * S + a * 128], axis=AX.X),
                    r=[esb_key], w=[esk(16 + a, 17 + a)])
                p.op("dve", lambda e, a=a: e.tensor_add(
                    es[:, 32 + a:32 + a + 1], es[:, a:a + 1], es[:, 16 + a:16 + a + 1]),
                    r=[esk(a, a + 1), esk(16 + a, 17 + a)], w=[esk(32 + a, 33 + a)])
            else:
                p.op("dve", lambda e: e.tensor_copy(es[:, 32:33], es[:, 0:1]),
                     r=[esk(0, 1)], w=[esk(32, 33)])
        p.op("dve", lambda e: e.reciprocal(es[:, 48:64], es[:, 32:48]),
             r=[esk(32, 48)], w=[esk(48, 64)])
        for b in range(NTS):
            t = base + b
            ring = t % 4
            xt_h, xt_b = c.xt[ring], c.xt_buf[ring]
            p.dma("sp", xt_h[:], xin_t[t], r=[(xin_buf, t, t + 1)], w=[xt_b])
            for half in range(2):
                bank = c.bank()
                pairs = []
                for a in range(b):
                    pairs.append((esb[:, a * S + b * 128: a * S + (b + 1) * 128],
                                  vsb[:, a * 1024 + half * 512: a * 1024 + (half + 1) * 512]))
                pairs.append((edmU[:, b * 128:(b + 1) * 128],
                              vsb[:, b * 1024 + half * 512: b * 1024 + (half + 1) * 512]))
                mm(c, bank, pairs, [esb_key, edmU_key, vsb_key])
                p.op("dve", lambda e, bk=bank[0], half=half, b=b: e.tensor_scalar_mul(
                    ysb[:, half * 512:(half + 1) * 512], bk[:], es[:, 48 + b:48 + b + 1]),
                    r=[(bank[1], 0, 2048), esk(48 + b, 49 + b)], w=[ysb_key])
            dst = yt[:].rearrange("p (j q) -> p j q", q=128)
            transpose_128(c, dst, yt_key, ysb[:], [ysb_key])
            for half in range(2):
                bank = c.bank()
                mm(c, bank,
                   [(yt[:, j * 128:(j + 1) * 128],
                     woT[:, j * 1024 + half * 512: j * 1024 + (half + 1) * 512])
                    for j in range(KD)]
                   + [(c.ones_row[0:1, 0:128], bo[0:1, half * 512:(half + 1) * 512])],
                   [yt_key, woT_key, bo_key, c.const_buf])
                p.op("dve", lambda e, bk=bank[0], xt=xt_h, half=half:
                     e.tensor_tensor(xt[:, half * 512:(half + 1) * 512], bk[:],
                                     xt[:, half * 512:(half + 1) * 512], ALU.add),
                     r=[(bank[1], 0, 2048), xt_b], w=[xt_b])
            p.dma("act", xout_t[t], xt_h[:], r=[xt_b], w=[(xout_buf, t, t + 1)])


def seg_l1_attn(c, xin, xin_buf, xout, xout_buf, W):
    p = c.p
    c.arena.reset()
    H, DH = 16, 64
    waT, waT_key = load_weight_T(c, W["L1_attn_w_attn"], 3072, 1024, "wa")
    wpT, wpT_key = load_weight_T(c, W["L1_attn_w_proj"], 1024, 1024, "wp")
    inv = 1.0 / (10000.0 ** (np.arange(0, DH, 2) / DH))
    tpos = np.arange(64)[:, None] * inv[None, :]
    cosn = np.cos(np.concatenate([tpos, tpos], 0)).astype(np.float32)
    sinn = np.sin(np.concatenate([tpos, tpos], 0)).astype(np.float32)
    cos_dr = c.nc.inline_tensor(cosn, name="l1cos")
    sin_dr = c.nc.inline_tensor(sinn, name="l1sin")
    cosb, cos_key = c.arena.alloc([128, 32], F32)
    sinb, sin_key = c.arena.alloc([128, 32], F32)
    p.dma_priv("sp", cosb[:], cos_dr.ap(), w=[cos_key])
    p.dma_priv("sp", sinb[:], sin_dr.ap(), w=[sin_key])
    bd01_dr = c.nc.inline_tensor(
        np.kron(np.eye(2), np.ones((64, 64))).astype(BF), name="bd01")
    bd01, bd_key = c.arena.alloc([128, 128], BF16)
    p.dma_priv("sp", bd01[:], bd01_dr.ap(), w=[bd_key])

    xnt, xnt_key = c.arena.alloc([128, 1024], BF16)
    qsb, q_key = c.arena.alloc([128, 1024], BF16)
    ksb, k_key = c.arena.alloc([128, 1024], BF16)
    vsb, v_key = c.arena.alloc([128, 1024], BFR16)
    rq, rq_key = c.arena.alloc([128, 1024], BF16)
    rk, rk_key = c.arena.alloc([128, 1024], BF16)
    tmp, tmp_key = c.arena.alloc([128, 512], BF16)
    rqT, rqT_key = c.arena.alloc([128, 1024], BF16)
    rkT, rkT_key = c.arena.alloc([128, 1024], BF16)
    em, em_key = c.arena.alloc([128, 16 * 128], BFR16)
    et, et_key = c.arena.alloc([128, 16 * 128], BFR16)
    ysb, y_key = c.arena.alloc([128, 1024], BF16)
    yt, yt_key = c.arena.alloc([128, 1024], BF16)
    sm, sm_key = c.arena.alloc([128, 32], F32)
    xin_t = xin.rearrange("(t p) c -> t p c", p=128)
    xout_t = xout.rearrange("(t p) c -> t p c", p=128)

    def v3(buf, off):
        return bass.AP(buf.tensor if isinstance(buf, bass.AP) else buf, 0, []) if False else \
            buf[:].rearrange("p (h d) -> p h d", d=64)[:, :, off:off + 32]

    cosB = cosb[:].unsqueeze(1).broadcast_to([128, 16, 32])
    sinB = sinb[:].unsqueeze(1).broadcast_to([128, 16, 32])

    def rope(src, src_key, dst, dst_key):
        s1, s2 = v3(src, 0), v3(src, 32)
        d1, d2 = v3(dst, 0), v3(dst, 32)
        tm = tmp[:].rearrange("p (h d) -> p h d", d=32)
        p.op("dve", lambda e: e.tensor_mul(d1, s1, cosB), r=[src_key, cos_key], w=[dst_key])
        p.op("dve", lambda e: e.tensor_mul(tm, s2, sinB), r=[src_key, sin_key], w=[tmp_key])
        p.op("dve", lambda e: e.tensor_add(d1, d1, tm), r=[dst_key, tmp_key], w=[dst_key])
        p.op("dve", lambda e: e.tensor_mul(d2, s2, cosB), r=[src_key, cos_key], w=[dst_key])
        p.op("dve", lambda e: e.tensor_mul(tm, s1, sinB), r=[src_key, sin_key], w=[tmp_key])
        p.op("dve", lambda e: e.tensor_sub(d2, d2, tm), r=[dst_key, tmp_key], w=[dst_key])

    for t in range(NT):
        ring = t % 4
        xt_h, xt_b = c.xt[ring], c.xt_buf[ring]
        p.dma("sp", xt_h[:], xin_t[t], r=[(xin_buf, t, t + 1)], w=[xt_b])
        xn_h, xn_b = c.xn[ring], c.xn_buf[ring]
        ln_tile(c, xt_h, xt_b, xn_h, xn_b, s0=4 * ring)
        transpose_128(c, xnt[:].rearrange("p (j q) -> p j q", q=128), xnt_key,
                      xn_h[:], [xn_b])
        for part, (dst, dkey) in enumerate(((qsb, q_key), (ksb, k_key), (vsb, v_key))):
            for half in range(2):
                o = part * 1024 + half * 512
                bank = c.bank()
                mm(c, bank,
                   [(xnt[:, j * 128:(j + 1) * 128], waT[:, j * 3072 + o: j * 3072 + o + 512])
                    for j in range(KD)],
                   [xnt_key, waT_key])
                p.op("dve", lambda e, b=bank[0], dst=dst, half=half: e.tensor_copy(
                    dst[:, half * 512:(half + 1) * 512], b[:]),
                    r=[(bank[1], 0, 2048)], w=[dkey])
        rope(qsb, q_key, rq, rq_key)
        rope(ksb, k_key, rk, rk_key)
        transpose_128(c, rqT[:].rearrange("p (j q) -> p j q", q=128), rqT_key,
                      rq[:], [rq_key])
        transpose_128(c, rkT[:].rearrange("p (j q) -> p j q", q=128), rkT_key,
                      rk[:], [rk_key])
        for h in range(H):
            j, pb = h // 2, (h % 2) * 64
            bank = c.bank()
            mm(c, bank, [(rqT[pb:pb + 64, j * 128:(j + 1) * 128],
                          rkT[pb:pb + 64, j * 128:(j + 1) * 128])],
               [rqT_key, rkT_key], n_lo=0, n_hi=128)
            p.op("act", lambda e, b=bank[0], h=h: e.activation(
                em[:, h * 128:(h + 1) * 128], b[:, 0:128], AF.Exp, scale=0.125),
                r=[(bank[1], 0, 512)], w=[(c.arena.buf, em_key[1] + h * 256,
                                           em_key[1] + (h + 1) * 256)])
            p.op("dve", lambda e, h=h: e.scalar_tensor_tensor(
                em[:, h * 128:(h + 1) * 128], em[:, h * 128:(h + 1) * 128], 1.0,
                bd01[:], ALU.mult, ALU.mult, accum_out=sm[:, h:h + 1]),
                r=[(c.arena.buf, em_key[1] + h * 256, em_key[1] + (h + 1) * 256), bd_key],
                w=[(c.arena.buf, em_key[1] + h * 256, em_key[1] + (h + 1) * 256),
                   (c.arena.buf, sm_key[1] + h * 4, sm_key[1] + (h + 1) * 4)])
        p.op("dve", lambda e: e.reciprocal(sm[:, 16:32], sm[:, 0:16]),
             r=[(c.arena.buf, sm_key[1], sm_key[1] + 64)],
             w=[(c.arena.buf, sm_key[1] + 64, sm_key[1] + 128)])
        for g in range(2):
            bank, bank_buf = c.bank()
            bb = bank.bitcast(BFR16)

            def f(e, g=g, bb=bb):
                last = None
                for j in range(8):
                    last = e.transpose(bb[:, j * 128:(j + 1) * 128],
                                       em[:, (g * 8 + j) * 128:(g * 8 + j + 1) * 128],
                                       c.ident[:])
                return last
            p.op("pe", f, r=[em_key, c.const_buf], w=[(bank_buf, 0, 1 << 30)])
            p.op("dve", lambda e, g=g, bb=bb: e.tensor_copy(
                et[:, g * 1024:(g + 1) * 1024], bb[:, 0:1024]),
                r=[(bank_buf, 0, 1 << 30)], w=[et_key])
        for g in range(2):
            bank, bank_buf = c.bank()

            def fo(e, b=bank, g=g):
                last = None
                for i in range(8):
                    h = g * 8 + i
                    last = e.matmul(b[:, i * 64:(i + 1) * 64],
                                    et[:, h * 128:(h + 1) * 128],
                                    vsb[:, h * 64:(h + 1) * 64],
                                    start=True, stop=True)
                return last
            p.op("pe", fo, r=[et_key, v_key], w=[(bank_buf, 0, 1 << 30)])
            for i in range(8):
                h = g * 8 + i
                p.op("dve", lambda e, b=bank, h=h, i=i: e.tensor_scalar_mul(
                    ysb[:, h * 64:(h + 1) * 64], b[:, i * 64:(i + 1) * 64],
                    sm[:, 16 + h:16 + h + 1]),
                    r=[(bank_buf, 0, 1 << 30),
                       (c.arena.buf, sm_key[1] + 64, sm_key[1] + 128)],
                    w=[y_key])
        transpose_128(c, yt[:].rearrange("p (j q) -> p j q", q=128), yt_key,
                      ysb[:], [y_key])
        for half in range(2):
            bank = c.bank()
            mm(c, bank,
               [(yt[:, j * 128:(j + 1) * 128],
                 wpT[:, j * 1024 + half * 512: j * 1024 + (half + 1) * 512])
                for j in range(KD)],
               [yt_key, wpT_key])
            p.op("dve", lambda e, bk=bank[0], xt=xt_h, half=half:
                 e.tensor_tensor(xt[:, half * 512:(half + 1) * 512], bk[:],
                                 xt[:, half * 512:(half + 1) * 512], ALU.add),
                 r=[(bank[1], 0, 2048), xt_b], w=[xt_b])
        p.dma("act", xout_t[t], xt_h[:], r=[xt_b], w=[(xout_buf, t, t + 1)])


def _win_tables(window, dh=64):
    n = S
    half = np.arange(0, dh, 2, dtype=np.float64)
    inv_freq = 1.0 / 10000.0 ** (half / dh)
    t = np.arange(n, dtype=np.float64)
    freqs = t[:, None] * inv_freq[None, :]
    freqs = np.concatenate([freqs, freqs], -1)
    base = (half + 0.4 * dh) / (1.4 * dh)
    power = (t - n // 2) / (window // 2)
    scale = base[None, :] ** power[:, None]
    scale = np.concatenate([scale, scale], -1)
    cos, sin = np.cos(freqs), np.sin(freqs)
    return (cos * scale, sin * scale, cos / scale, sin / scale)


def seg_win_ffn(c, xin, xin_buf, xout, xout_buf, W, cfg):
    p = c.p
    nc = c.nc
    H = cfg["heads"]
    INNER = H * 64
    NI = INNER // 128
    WIN = cfg["window"]
    LB = WIN // 128
    HID2 = cfg["hid2"]
    HID = HID2 // 2
    NH2 = (HID + 127) // 128
    DEPTH = cfg["depth"]
    qkmul = 8.0
    ua = nc.dram_tensor(cfg["tag"] + "_ua", [T, D], F32, kind="Internal").ap()
    ub = nc.dram_tensor(cfg["tag"] + "_ub", [T, D], F32, kind="Internal").ap()
    ua_buf, ub_buf = Buf("ua"), Buf("ub")
    xin_t = xin.rearrange("(t p) c -> t p c", p=128)
    xout_t = xout.rearrange("(t p) c -> t p c", p=128)

    c.arena.reset()
    csc, snc, csi, sni = _win_tables(WIN)
    tab_dr = nc.inline_tensor(
        np.stack([csc, snc, csi, sni]).astype(np.float32), name=cfg["tag"] + "_tab")
    qs_rep, qs_rep_key = c.arena.alloc([128, 2 * 64], F32)
    mskU_dr = nc.inline_tensor(
        np.where(np.arange(128)[:, None] <= np.arange(128)[None, :], 0.0,
                 -30000.0).astype(BFR), name=cfg["tag"] + "_mU")
    mskL_dr = nc.inline_tensor(
        np.where(np.arange(128)[:, None] >= np.arange(128)[None, :], 0.0,
                 -30000.0).astype(BFR), name=cfg["tag"] + "_mL")
    mskU, mskU_key = c.arena.alloc([128, 128], BFR16)
    mskL, mskL_key = c.arena.alloc([128, 128], BFR16)
    p.dma_priv("sp", mskU[:], mskU_dr.ap(), w=[mskU_key])
    p.dma_priv("sp", mskL[:], mskL_dr.ap(), w=[mskL_key])
    seg_base = c.arena.cur

    def ln_outer():
        for t in range(NT):
            ring = t % 4
            xt_h, xt_b = c.xt[ring], c.xt_buf[ring]
            p.dma("sp", xt_h[:], xin_t[t], r=[(xin_buf, t, t + 1)], w=[xt_b])
            ln_tile(c, xt_h, xt_b, xt_h, xt_b, s0=4 * ring)
            p.dma("act", ua.rearrange("(t p) c -> t p c", p=128)[t], xt_h[:],
                  r=[xt_b], w=[(ua_buf, t, t + 1)])

    def ln_tile_f32(xt_h, xt_b, s0):
        sum_ap, sum_k = c.stat_slot(s0)
        nm_ap, nm_k = c.stat_slot(s0 + 1)
        var_ap, var_k = c.stat_slot(s0 + 2)
        rs_ap, rs_k = c.stat_slot(s0 + 3)
        p.op("dve", lambda e: e.reduce_sum(sum_ap, xt_h[:], axis=AX.X),
             r=[xt_b], w=[sum_k])
        p.op("dve", lambda e: e.tensor_scalar_mul(nm_ap, sum_ap, -1.0 / D),
             r=[sum_k], w=[nm_k])
        p.op("act", lambda e: e.activation(c.sq[:, 0:D], xt_h[:], AF.Square,
                                           bias=nm_ap, accum_out=var_ap),
             r=[xt_b, nm_k], w=[c.sq_buf, var_k])
        eps_ap = c.constf(1e-5)
        p.op("act", lambda e: e.activation(rs_ap, var_ap, AF.Sqrt,
                                           scale=1.0 / D, bias=eps_ap),
             r=[var_k, c.const_buf], w=[rs_k])
        p.op("dve", lambda e: e.reciprocal(rs_ap, rs_ap), r=[rs_k], w=[rs_k])
        p.op("dve", lambda e: e.tensor_scalar(xt_h[:], xt_h[:],
                                              nm_ap, rs_ap, ALU.add, ALU.mult),
             r=[xt_b, nm_k, rs_k], w=[xt_b])

    def ln_outer2():
        for t in range(NT):
            ring = t % 4
            xt_h, xt_b = c.xt[ring], c.xt_buf[ring]
            p.dma("sp", xt_h[:], xin_t[t], r=[(xin_buf, t, t + 1)], w=[xt_b])
            ln_tile_f32(xt_h, xt_b, 4 * ring)
            p.dma("act", ua.rearrange("(t p) c -> t p c", p=128)[t], xt_h[:],
                  r=[xt_b], w=[(ua_buf, t, t + 1)])

    def attn_phase(li, src, src_buf, dst, dst_buf):
        c.arena.cur = seg_base
        pre = cfg["pre"]
        wqkv_dr = W[pre("Wqkv")][li]
        wo_dr = W[pre("Wo")][li]
        wqkv, wqkv_key = c.arena.alloc([128, KD * 3 * INNER], BF16)
        p.dma("pool", wqkv[:].rearrange("p (k n) -> p k n", n=3 * INNER),
              wqkv_dr.rearrange("(k p) n -> p k n", p=128), w=[wqkv_key])
        wo, wo_key = c.arena.alloc([128, NI * D], BF16)
        p.dma("pool", wo[:].rearrange("p (k n) -> p k n", n=D),
              wo_dr.rearrange("(k p) n -> p k n", p=128), w=[wo_key])
        tabs, tabs_key = c.arena.alloc([128, 4 * NTS * 64], F32)
        qsr, qsr_key = stage_row(c, W[pre("q_scale")][li], 64)
        ksr, ksr_key = stage_row(c, W[pre("k_scale")][li], 64)
        bank, bank_buf = c.bank()
        p.op("pe", lambda e, b=bank: e.matmul(b[:, 0:64], c.ones_row[0:1, 0:128],
                                              qsr, start=True, stop=True),
             r=[qsr_key, c.const_buf], w=[(bank_buf, 0, 1 << 30)])
        p.op("pe", lambda e, b=bank: e.matmul(b[:, 64:128], c.ones_row[0:1, 0:128],
                                              ksr, start=False, stop=True),
             r=[ksr_key, c.const_buf], w=[(bank_buf, 0, 1 << 30)])
        p.op("dve", lambda e, b=bank: e.tensor_copy(qs_rep[:], b[:, 0:128]),
             r=[(bank_buf, 0, 1 << 30)], w=[qs_rep_key])
        tab_src = tab_dr.ap().rearrange("f (t p) d -> p f t d", p=128)
        p.dma_priv("sp", tabs[:].rearrange("p (f t d) -> p f t d", t=NTS, d=64),
                   tab_src, w=[tabs_key])
        t3 = tabs[:].rearrange("p (f td) -> p f td", f=4)
        p.op("dve", lambda e: e.tensor_scalar_mul(
            tabs[:, 0:2 * NTS * 64], tabs[:, 0:2 * NTS * 64], qkmul),
            r=[tabs_key], w=[tabs_key])
        sxb, sx_key = c.arena.alloc([128, D], BF16)
        sxT, sxT_key = c.arena.alloc([128, D], BF16)
        qsb, q_key = c.arena.alloc([128, INNER], BFR16)
        ksb, k_key = c.arena.alloc([128, INNER], BFR16)
        rq, rq_key = c.arena.alloc([128, INNER], BFR16)
        rk, rk_key = c.arena.alloc([128, INNER], BFR16)
        tmp, tmp_key = c.arena.alloc([128, INNER], BFR16)
        rqT, rqT_key = c.arena.alloc([128, INNER], BFR16)
        ss, ss_key = c.arena.alloc([128, 8 * H], F32)
        NRING = LB + 2
        kT_ring = [c.arena.alloc([128, INNER], BFR16) for _ in range(NRING)]
        v_ring = [c.arena.alloc([128, INNER], BF16) for _ in range(NRING)]
        esb, e_key = c.arena.alloc([128, (LB + 1) * 128], BF16)
        etb, et_key = c.arena.alloc([128, (LB + 1) * 128], BF16)
        ysb, y_key = c.arena.alloc([128, INNER], BF16)
        yt, yt_key = c.arena.alloc([128, INNER], BF16)
        src_t = src.rearrange("(t p) c -> t p c", p=128)
        dst_t = dst.rearrange("(t p) c -> t p c", p=128)

        def ssk(lo, hi):
            return (c.arena.buf, ss_key[1] + lo * 4, ss_key[1] + hi * 4)

        def l2n_rope(xsb, x_key, dstb, dst_key, f0, tl):
            p.op("act", lambda e: e.activation(c.sq[:, 0:INNER], xsb[:], AF.Square),
                 r=[x_key], w=[c.sq_buf])
            p.op("dve", lambda e: e.reduce_sum(
                ss[:, 0:H], c.sq[:, 0:INNER].rearrange("p (h d) -> p h d", d=64),
                axis=AX.X), r=[c.sq_buf], w=[ssk(0, H)])
            p.op("act", lambda e: e.activation(ss[:, H:2 * H], ss[:, 0:H], AF.Sqrt),
                 r=[ssk(0, H)], w=[ssk(H, 2 * H)])
            p.op("dve", lambda e: e.tensor_scalar_max(
                ss[:, H:2 * H], ss[:, H:2 * H], 1e-12), r=[ssk(H, 2 * H)],
                w=[ssk(H, 2 * H)])
            p.op("dve", lambda e: e.reciprocal(ss[:, 2 * H:3 * H], ss[:, H:2 * H]),
                 r=[ssk(H, 2 * H)], w=[ssk(2 * H, 3 * H)])
            rs3 = ss[:, 2 * H:3 * H].unsqueeze(2).broadcast_to([128, H, 64])
            x3 = xsb[:].rearrange("p (h d) -> p h d", d=64)
            p.op("dve", lambda e: e.tensor_tensor(x3, x3, rs3, ALU.mult),
                 r=[x_key, ssk(2 * H, 3 * H)], w=[x_key])
            qsB = (qs_rep[:, 0:64] if f0 == 0 else qs_rep[:, 64:128]
                   ).unsqueeze(1).broadcast_to([128, H, 64])
            p.op("dve", lambda e, qsB=qsB: e.tensor_tensor(x3, x3, qsB, ALU.mult),
                 r=[x_key, qs_rep_key], w=[x_key])
            cs = tabs[:, (f0 * NTS + tl) * 64:(f0 * NTS + tl) * 64 + 64]
            sn = tabs[:, ((f0 + 1) * NTS + tl) * 64:((f0 + 1) * NTS + tl) * 64 + 64]

            def vv(b, off, wdt=32):
                return b[:].rearrange("p (h d) -> p h d", d=64)[:, :, off:off + wdt]
            cs1 = cs[:, 0:32].unsqueeze(1).broadcast_to([128, H, 32])
            cs2 = cs[:, 32:64].unsqueeze(1).broadcast_to([128, H, 32])
            sn1 = sn[:, 0:32].unsqueeze(1).broadcast_to([128, H, 32])
            sn2 = sn[:, 32:64].unsqueeze(1).broadcast_to([128, H, 32])
            tm = tmp[:].rearrange("p (h d) -> p h d", d=64)
            s1, s2 = vv(xsb, 0), vv(xsb, 32)
            d1, d2 = vv(dstb, 0), vv(dstb, 32)
            p.op("dve", lambda e: e.tensor_mul(d1, s1, cs1), r=[x_key, tabs_key], w=[dst_key])
            p.op("dve", lambda e: e.tensor_mul(tm[:, :, 0:32], s2, sn1),
                 r=[x_key, tabs_key], w=[tmp_key])
            p.op("dve", lambda e: e.tensor_sub(d1, d1, tm[:, :, 0:32]),
                 r=[dst_key, tmp_key], w=[dst_key])
            p.op("dve", lambda e: e.tensor_mul(d2, s2, cs2), r=[x_key, tabs_key], w=[dst_key])
            p.op("dve", lambda e: e.tensor_mul(tm[:, :, 0:32], s1, sn2),
                 r=[x_key, tabs_key], w=[tmp_key])
            p.op("dve", lambda e: e.tensor_add(d2, d2, tm[:, :, 0:32]),
                 r=[dst_key, tmp_key], w=[dst_key])

        for sq in range(BPC):
            for tl in range(NTS):
                t = sq * NTS + tl
                slot = tl % NRING
                p.dma("pool", sxb[:, 0:512], src_t[t][:, 0:512],
                      r=[(src_buf, t, t + 1)], w=[sx_key])
                if tl == 0:
                    p.op("pool", lambda e: e.memset(sxb[0:1, 512:1024], 0.0),
                         w=[sx_key])
                    p.dma("pool", sxb[1:128, 512:1024],
                          src.rearrange("r c -> r c")[t * 128:t * 128 + 127, 512:1024],
                          r=[(src_buf, t, t + 1)], w=[sx_key])
                else:
                    p.dma("pool", sxb[:, 512:1024],
                          src.rearrange("r c -> r c")[t * 128 - 1:t * 128 + 127, 512:1024],
                          r=[(src_buf, t - 1, t + 1)], w=[sx_key])
                transpose_128(c, sxT[:].rearrange("p (j q) -> p j q", q=128),
                              sxT_key, sxb[:], [sx_key])
                vh, vk = v_ring[slot]
                for part, (dstb, dkey) in enumerate(
                        ((qsb, q_key), (ksb, k_key), (vh, vk))):
                    nchunk = (INNER + 511) // 512
                    for ch in range(nchunk):
                        o = part * INNER + ch * 512
                        wdt = min(512, INNER - ch * 512)
                        bank = c.bank()
                        mm(c, bank,
                           [(sxT[:, j * 128:(j + 1) * 128],
                             wqkv[:, j * 3 * INNER + o: j * 3 * INNER + o + wdt])
                            for j in range(KD)],
                           [sxT_key, wqkv_key])
                        p.op("dve", lambda e, b=bank[0], dstb=dstb, ch=ch, wdt=wdt:
                             e.tensor_copy(dstb[:, ch * 512:ch * 512 + wdt],
                                           b[:, 0:wdt]),
                             r=[(bank[1], 0, 1 << 30)], w=[dkey])
                l2n_rope(qsb, q_key, rq, rq_key, 0, tl)
                l2n_rope(ksb, k_key, rk, rk_key, 2, tl)
                if _DBG is not None and li == 0 and sq == 0 and 4 <= tl <= 6:
                    dslot = (tl - 4) * 4
                    for ii, (bb, kk2) in enumerate(((rq, rq_key), (rk, rk_key),
                                                    (vh, vk), (sxb, sx_key))):
                        p.dma("pool", _DBG[0][dslot + ii], bb[:],
                              r=[kk2], w=[(_DBG[1], dslot + ii, dslot + ii + 1)])
                transpose_128(c, rqT[:].rearrange("p (j q) -> p j q", q=128),
                              rqT_key, rq[:], [rq_key], n_in=NI)
                kTh, kTk = kT_ring[slot]
                transpose_128(c, kTh[:].rearrange("p (j q) -> p j q", q=128),
                              kTk, rk[:], [rk_key], n_in=NI)
                blocks = [bi for bi in range(tl - LB, tl + 1) if bi >= 0]
                nb = len(blocks)
                for h in range(H):
                    j, pb = h // 2, (h % 2) * 64
                    groups = [blocks[i:i + 4] for i in range(0, nb, 4)]
                    gbanks = []
                    for gi, grp in enumerate(groups):
                        bank = c.bank()
                        gbanks.append(bank)

                        def fS(e, b=bank[0], h=h, j=j, pb=pb, grp=grp, tl=tl):
                            last = None
                            for bi_i, bi in enumerate(grp):
                                sl = kT_ring[bi % NRING][0]
                                mk = None
                                if bi == tl - LB:
                                    mk = mskU
                                elif bi == tl:
                                    mk = mskL
                                last = e.matmul(
                                    b[:, bi_i * 128:(bi_i + 1) * 128],
                                    rqT[pb:pb + 64, j * 128:(j + 1) * 128],
                                    sl[pb:pb + 64, j * 128:(j + 1) * 128],
                                    start=True, stop=(mk is None))
                                if mk is not None:
                                    last = e.matmul(
                                        b[:, bi_i * 128:(bi_i + 1) * 128],
                                        c.ident[:], mk[:], start=False, stop=True)
                            return last
                        rdeps = [rqT_key, mskU_key, mskL_key, c.const_buf]
                        for bi in grp:
                            rdeps.append(kT_ring[bi % NRING][1])
                        p.op("pe", fS, r=rdeps, w=[(bank[1], 0, 1 << 30)])
                        if gi == 0:
                            p.op("dve", lambda e, b=bank[0], ng=len(grp), h=h:
                                 e.reduce_max(ss[:, 3 * H + h:3 * H + h + 1],
                                              b[:, 0:ng * 128], axis=AX.X, negate=True),
                                 r=[(bank[1], 0, 1 << 30)],
                                 w=[ssk(3 * H + h, 3 * H + h + 1)])
                        else:
                            p.op("dve", lambda e, b=bank[0], ng=len(grp), h=h:
                                 e.reduce_max(ss[:, 6 * H + h:6 * H + h + 1],
                                              b[:, 0:ng * 128], axis=AX.X, negate=True),
                                 r=[(bank[1], 0, 1 << 30)],
                                 w=[ssk(6 * H + h, 6 * H + h + 1)])
                            p.op("dve", lambda e, h=h: e.tensor_tensor(
                                ss[:, 3 * H + h:3 * H + h + 1],
                                ss[:, 3 * H + h:3 * H + h + 1],
                                ss[:, 6 * H + h:6 * H + h + 1], ALU.min),
                                r=[ssk(3 * H + h, 3 * H + h + 1),
                                   ssk(6 * H + h, 6 * H + h + 1)],
                                w=[ssk(3 * H + h, 3 * H + h + 1)])
                    off = 0
                    for gi, grp in enumerate(groups):
                        bank = gbanks[gi]
                        ng = len(grp)
                        p.op("act", lambda e, b=bank[0], ng=ng, h=h, off=off, gi=gi:
                             e.activation(esb[:, off * 128:(off + ng) * 128],
                                          b[:, 0:ng * 128], AF.Exp,
                                          bias=ss[:, 3 * H + h:3 * H + h + 1],
                                          accum_out=ss[:, (4 if gi == 0 else 7) * H + h:
                                                       (4 if gi == 0 else 7) * H + h + 1]),
                             r=[(bank[1], 0, 1 << 30), ssk(3 * H + h, 3 * H + h + 1)],
                             w=[e_key, ssk((4 if gi == 0 else 7) * H + h,
                                           (4 if gi == 0 else 7) * H + h + 1)])
                        off += ng
                    if len(groups) > 1:
                        p.op("dve", lambda e, h=h: e.tensor_add(
                            ss[:, 4 * H + h:4 * H + h + 1],
                            ss[:, 4 * H + h:4 * H + h + 1],
                            ss[:, 7 * H + h:7 * H + h + 1]),
                            r=[ssk(4 * H + h, 4 * H + h + 1), ssk(7 * H + h, 7 * H + h + 1)],
                            w=[ssk(4 * H + h, 4 * H + h + 1)])
                    p.op("dve", lambda e, h=h: e.reciprocal(
                        ss[:, 5 * H + h:5 * H + h + 1], ss[:, 4 * H + h:4 * H + h + 1]),
                        r=[ssk(4 * H + h, 4 * H + h + 1)],
                        w=[ssk(5 * H + h, 5 * H + h + 1)])
                    if _DBG is not None and li == 0 and sq == 0 and tl == 5 and h == 3:
                        p.dma("pool", _DBG[0][12][:, 0:nb * 128], esb[:, 0:nb * 128],
                              r=[e_key], w=[(_DBG[1], 12, 13)])
                        p.dma("pool", _DBG[0][13][:, 0:2], ss[:, 4 * H + h:4 * H + h + 2],
                              r=[ssk(4 * H + h, 4 * H + h + 2)], w=[(_DBG[1], 13, 14)])
                    bank2, bank2_buf = c.bank()
                    bb2 = bank2.bitcast(BF16)

                    def fT(e, bb2=bb2, nb=nb):
                        last = None
                        for bi_i in range(nb):
                            last = e.transpose(bb2[:, bi_i * 128:(bi_i + 1) * 128],
                                               esb[:, bi_i * 128:(bi_i + 1) * 128],
                                               c.identh[:])
                        return last
                    p.op("pe", fT, r=[e_key, c.const_buf], w=[(bank2_buf, 0, 1 << 30)])
                    p.op("dve", lambda e, bb2=bb2, nb=nb: e.tensor_copy(
                        etb[:, 0:nb * 128], bb2[:, 0:nb * 128]),
                        r=[(bank2_buf, 0, 1 << 30)], w=[et_key])
                    bank3 = c.bank()

                    def fO(e, b=bank3[0], h=h, blocks=blocks):
                        last = None
                        for bi_i, bi in enumerate(blocks):
                            vsl = v_ring[bi % NRING][0]
                            last = e.matmul(b[:, 0:64],
                                            etb[:, bi_i * 128:(bi_i + 1) * 128],
                                            vsl[:, h * 64:(h + 1) * 64],
                                            start=(bi_i == 0),
                                            stop=(bi_i == len(blocks) - 1))
                        return last
                    rdeps3 = [et_key] + [v_ring[bi % NRING][1] for bi in blocks]
                    p.op("pe", fO, r=rdeps3, w=[(bank3[1], 0, 1 << 30)])
                    p.op("dve", lambda e, b=bank3[0], h=h: e.tensor_scalar_mul(
                        ysb[:, h * 64:(h + 1) * 64], b[:, 0:64],
                        ss[:, 5 * H + h:5 * H + h + 1]),
                        r=[(bank3[1], 0, 1 << 30), ssk(5 * H + h, 5 * H + h + 1)],
                        w=[y_key])
                transpose_128(c, yt[:].rearrange("p (j q) -> p j q", q=128),
                              yt_key, ysb[:], [y_key], n_in=NI)
                ring = t % 4
                xt_h, xt_b = c.xt[ring], c.xt_buf[ring]
                p.dma("sp", xt_h[:], src_t[t], r=[(src_buf, t, t + 1)], w=[xt_b])
                for half in range(2):
                    bank = c.bank()
                    mm(c, bank,
                       [(yt[:, j * 128:(j + 1) * 128],
                         wo[:, j * D + half * 512: j * D + (half + 1) * 512])
                        for j in range(NI)],
                       [yt_key, wo_key])
                    p.op("dve", lambda e, bk=bank[0], xt=xt_h, half=half:
                         e.tensor_tensor(xt[:, half * 512:(half + 1) * 512], bk[:],
                                         xt[:, half * 512:(half + 1) * 512], ALU.add),
                         r=[(bank[1], 0, 1 << 30), xt_b], w=[xt_b])
                p.dma("act", dst_t[t], xt_h[:], r=[xt_b], w=[(dst_buf, t, t + 1)])

    def ffn_phase(li, src, src_buf, dst, dst_buf, add_xin):
        c.arena.cur = seg_base
        pre = cfg["pre"]
        w1_dr = W[pre("W1")][li]
        w2_dr = W[pre("W2")][li]
        w1, w1_key = c.arena.alloc([128, KD * HID2], BF16)
        p.dma("pool", w1[:].rearrange("p (k n) -> p k n", n=HID2),
              w1_dr.rearrange("(k p) n -> p k n", p=128), w=[w1_key])
        w2, w2_key = c.arena.alloc([128, NH2 * D], BF16)
        nfull = HID // 128
        p.dma("pool", w2[:, 0:nfull * D].rearrange("p (k n) -> p k n", n=D),
              w2_dr[0:nfull * 128].rearrange("(k p) n -> p k n", p=128), w=[w2_key])
        rem = HID - nfull * 128
        if rem:
            p.dma("pool", w2[0:rem, nfull * D:(nfull + 1) * D],
                  w2_dr[nfull * 128:HID], w=[w2_key])
        gcol, gcol_key = c.arena.alloc([128, KD], F32)
        p.dma_priv("pool", gcol[:],
                   W[pre("ln_g")][li].rearrange("(j p) -> p j", p=128),
                   w=[gcol_key], allow_slow_non_contiguous=True)
        bcol, bcol_key = c.arena.alloc([128, KD], F32)
        p.dma_priv("pool", bcol[:],
                   W[pre("ln_b")][li].rearrange("(j p) -> p j", p=128),
                   w=[bcol_key], allow_slow_non_contiguous=True)
        bcolb, bcolb_key = c.arena.alloc([128, KD], BF16)
        p.op("dve", lambda e: e.tensor_copy(bcolb[:], bcol[:]), r=[bcol_key],
             w=[bcolb_key])
        b1r, b1r_key = stage_row(c, W[pre("b1")][li], HID2)
        b2r, b2r_key = stage_row(c, W[pre("b2")][li], D)
        brow, brow_key = c.arena.alloc([1, HID2], BF16)
        for ch in range((HID2 + 511) // 512):
            wdt = min(512, HID2 - ch * 512)
            bank = c.bank()
            mm(c, bank,
               [(bcolb[:, j:j + 1], w1[:, j * HID2 + ch * 512: j * HID2 + ch * 512 + wdt])
                for j in range(KD)]
               + [(c.ones_row[0:1, 0:1], b1r[0:1, ch * 512:ch * 512 + wdt])],
               [bcolb_key, w1_key, b1r_key, c.const_buf], n_hi=wdt, m_out=1)
            p.op("dve", lambda e, b=bank[0], ch=ch, wdt=wdt: e.tensor_copy(
                brow[0:1, ch * 512:ch * 512 + wdt], b[0:1, 0:wdt]),
                r=[(bank[1], 0, 1 << 30)], w=[brow_key])
        for j in range(KD):
            p.op("dve", lambda e, j=j: e.tensor_scalar_mul(
                w1[:, j * HID2:(j + 1) * HID2], w1[:, j * HID2:(j + 1) * HID2],
                gcol[:, j:j + 1]), r=[w1_key, gcol_key], w=[w1_key])
        sxf, sxf_key = c.arena.alloc([128, D], F32)
        hg, hg_key = c.arena.alloc([128, HID], BF16)
        hgT, hgT_key = c.arena.alloc([128, NH2 * 128], BF16)
        gel, gel_key = c.arena.alloc([128, 512], BF16)
        src_t = src.rearrange("(t p) c -> t p c", p=128)
        dst_t = dst.rearrange("(t p) c -> t p c", p=128)
        for sq in range(BPC):
            for tl in range(NTS):
                t = sq * NTS + tl
                p.dma("sp", sxf[:, 0:512], src_t[t][:, 0:512],
                      r=[(src_buf, t, t + 1)], w=[sxf_key])
                if tl == 0:
                    p.op("pool", lambda e: e.memset(sxf[0:1, 512:1024], 0.0),
                         w=[sxf_key])
                    p.dma("sp", sxf[1:128, 512:1024],
                          src[t * 128:t * 128 + 127, 512:1024],
                          r=[(src_buf, t, t + 1)], w=[sxf_key])
                else:
                    p.dma("sp", sxf[:, 512:1024],
                          src[t * 128 - 1:t * 128 + 127, 512:1024],
                          r=[(src_buf, t - 1, t + 1)], w=[sxf_key])
                ring = t % 4
                xn_h, xn_b = c.xn[ring], c.xn_buf[ring]
                ln_tile(c, sxf, sxf_key, xn_h, xn_b, s0=4 * ring)
                xnt_h, xnt_b = c.xnt2 if hasattr(c, "xnt2") else (None, None)
                transpose_128(c, c.sq[:, 0:1024].bitcast(BF16)[:, 0:1024]
                              .rearrange("p (j q) -> p j q", q=128)
                              if False else hgT[:, 0:1024].rearrange("p (j q) -> p j q", q=128),
                              hgT_key, xn_h[:], [xn_b])
                xnt_slices = hgT
                for ch in range((HID + 511) // 512):
                    wdt = min(512, HID - ch * 512)
                    bank_a = c.bank()
                    mm(c, bank_a,
                       [(xnt_slices[:, j * 128:(j + 1) * 128],
                         w1[:, j * HID2 + ch * 512: j * HID2 + ch * 512 + wdt])
                        for j in range(KD)]
                       + [(c.ones_row[0:1, 0:128], brow[0:1, ch * 512:ch * 512 + wdt])],
                       [hgT_key, w1_key, brow_key, c.const_buf], n_hi=wdt)
                    bank_g = c.bank()
                    mm(c, bank_g,
                       [(xnt_slices[:, j * 128:(j + 1) * 128],
                         w1[:, j * HID2 + HID + ch * 512: j * HID2 + HID + ch * 512 + wdt])
                        for j in range(KD)]
                       + [(c.ones_row[0:1, 0:128],
                           brow[0:1, HID + ch * 512: HID + ch * 512 + wdt])],
                       [hgT_key, w1_key, brow_key, c.const_buf], n_hi=wdt)
                    p.op("act", lambda e, b=bank_g[0], wdt=wdt: e.activation(
                        gel[:, 0:wdt], b[:, 0:wdt], AF.Gelu),
                        r=[(bank_g[1], 0, 1 << 30)], w=[gel_key])
                    p.op("dve", lambda e, ba=bank_a[0], ch=ch, wdt=wdt: e.tensor_tensor(
                        hg[:, ch * 512:ch * 512 + wdt], ba[:, 0:wdt], gel[:, 0:wdt],
                        ALU.mult),
                        r=[(bank_a[1], 0, 1 << 30), gel_key], w=[hg_key])
                nfullT = HID // 128
                for g0 in range(0, nfullT + (1 if rem else 0), 8):
                    gn = min(8, nfullT + (1 if rem else 0) - g0)
                    bank, bank_buf = c.bank()
                    bb = bank.bitcast(BF16)

                    def fH(e, g0=g0, gn=gn, bb=bb):
                        last = None
                        for j in range(gn):
                            wdt = 128 if (g0 + j) < nfullT else rem
                            last = e.transpose(bb[0:wdt, j * 128:(j + 1) * 128],
                                               hg[:, (g0 + j) * 128:(g0 + j) * 128 + wdt],
                                               c.identh[:])
                        return last
                    p.op("pe", fH, r=[hg_key, c.const_buf], w=[(bank_buf, 0, 1 << 30)])
                    p.op("dve", lambda e, g0=g0, gn=gn, bb=bb: e.tensor_copy(
                        hgT[:, g0 * 128:(g0 + gn) * 128], bb[:, 0:gn * 128]),
                        r=[(bank_buf, 0, 1 << 30)], w=[hgT_key])
                ring = t % 4
                xt_h, xt_b = c.xt[ring], c.xt_buf[ring]
                p.dma("sp", xt_h[:], src_t[t], r=[(src_buf, t, t + 1)], w=[xt_b])
                if add_xin:
                    xn2_h, xn2_b = c.xn[ring], c.xn_buf[ring]
                    pass
                for half in range(2):
                    bank = c.bank()
                    pairs = []
                    for k in range(nfullT):
                        pairs.append((hgT[:, k * 128:(k + 1) * 128],
                                      w2[:, k * D + half * 512: k * D + (half + 1) * 512]))
                    if rem:
                        pairs.append((hgT[0:rem, nfullT * 128:nfullT * 128 + 128][:, 0:128]
                                      if False else hgT[0:rem, nfullT * 128:(nfullT + 1) * 128],
                                      w2[0:rem, nfullT * D + half * 512: nfullT * D + (half + 1) * 512]))
                    pairs.append((c.ones_row[0:1, 0:128],
                                  b2r[0:1, half * 512:(half + 1) * 512]))
                    mm(c, bank, pairs, [hgT_key, w2_key, b2r_key, c.const_buf])
                    p.op("dve", lambda e, bk=bank[0], xt=xt_h, half=half:
                         e.tensor_tensor(xt[:, half * 512:(half + 1) * 512], bk[:],
                                         xt[:, half * 512:(half + 1) * 512], ALU.add),
                         r=[(bank[1], 0, 1 << 30), xt_b], w=[xt_b])
                if add_xin:
                    p.dma("sp", sxf[:], xin_t[t], r=[(xin_buf, t, t + 1)], w=[sxf_key])
                    p.op("dve", lambda e, xt=xt_h: e.tensor_add(
                        xt[:], xt[:], sxf[:]), r=[xt_b, sxf_key], w=[xt_b])
                p.dma("act", dst_t[t], xt_h[:], r=[xt_b], w=[(dst_buf, t, t + 1)])

    import os
    dbg_depth = int(os.environ.get("KWD", "0"))
    dbg_phase = os.environ.get("KWP", "")
    ln_outer2()
    bufs = [(ua, ua_buf), (ub, ub_buf)]
    cur = 0
    if dbg_depth:
        for li in range(dbg_depth):
            lastl = (li == dbg_depth - 1)
            attn_phase(li, bufs[cur][0], bufs[cur][1], bufs[1 - cur][0], bufs[1 - cur][1])
            cur = 1 - cur
            if lastl and dbg_phase == "a":
                break
            ffn_phase(li, bufs[cur][0], bufs[cur][1], bufs[1 - cur][0], bufs[1 - cur][1],
                      add_xin=False)
            cur = 1 - cur
        src_t = bufs[cur][0].rearrange("(t p) c -> t p c", p=128)
        xo_t = xout.rearrange("(t p) c -> t p c", p=128)
        for t in range(NT):
            ring = t % 4
            xt_h, xt_b = c.xt[ring], c.xt_buf[ring]
            p.dma("sp", xt_h[:], src_t[t], r=[(bufs[cur][1], t, t + 1)], w=[xt_b])
            p.dma("act", xo_t[t], xt_h[:], r=[xt_b], w=[(xout_buf, t, t + 1)])
        return
    for li in range(DEPTH):
        attn_phase(li, bufs[cur][0], bufs[cur][1], bufs[1 - cur][0], bufs[1 - cur][1])
        cur = 1 - cur
        last = (li == DEPTH - 1)
        if last:
            ffn_phase(li, bufs[cur][0], bufs[cur][1], xout, xout_buf, add_xin=True)
        else:
            ffn_phase(li, bufs[cur][0], bufs[cur][1], bufs[1 - cur][0], bufs[1 - cur][1],
                      add_xin=False)
            cur = 1 - cur


def seg_l0_ffn(c, xin, xin_buf, xout, xout_buf, W):
    cfg = dict(tag="l0f", depth=6, heads=16, window=256, inner=1024, hid2=5460,
               pre=lambda s: "L0_ffn_" + {"Wqkv": "Wqkv", "Wo": "Wo", "W1": "W1",
                                          "W2": "W2", "b1": "b1", "b2": "b2",
                                          "ln_g": "ln_g", "ln_b": "ln_b",
                                          "q_scale": "q_scale", "k_scale": "k_scale"}[s])
    seg_win_ffn(c, xin, xin_buf, xout, xout_buf, W, cfg)


def seg_l2_ffn(c, xin, xin_buf, xout, xout_buf, W):
    cfg = dict(tag="l2f", depth=2, heads=8, window=512, inner=512, hid2=5460,
               pre=lambda s: "L2_ffn_" + {"Wqkv": "wqkv", "Wo": "wo", "W1": "w1",
                                          "W2": "w2", "b1": "b1", "b2": "b2",
                                          "ln_g": "ln_g", "ln_b": "ln_b",
                                          "q_scale": "q_scale", "k_scale": "k_scale"}[s])
    seg_win_ffn(c, xin, xin_buf, xout, xout_buf, W, cfg)


def seg_l2_attn(c, xin, xin_buf, xout, xout_buf, W):
    p = c.p
    nc = c.nc
    c.arena.reset()
    DECAY, SHARP = 0.995, 5.0
    scale = 1.0 / 32.0
    wT = {}
    _raw = [c.arena.alloc([128, 1024], BF16) for _ in range(2)]
    for nm, dr in (("rv", "L2_attn_reg_val_w"), ("rq", "L2_attn_reg_q_w"),
                   ("mv", "L2_attn_mem_val_w"), ("mq", "L2_attn_mem_q_w")):
        wT[nm] = load_weight_T(c, W[dr], 1024, 1024, nm, raw=_raw)
    comb = W["L2_attn_comb_w"]
    for nm, lo in (("wh", 0), ("wr", 1024), ("wm", 2048)):
        wT[nm] = load_weight_T(c, comb[:, lo:lo + 1024], 1024, 1024, nm, raw=_raw)
    for nm in ("rq", "mq"):
        h, k = wT[nm]
        for j in range(KD):
            p.op("dve", lambda e, h=h, j=j: e.tensor_scalar_mul(
                h[:, j * 1024:(j + 1) * 1024], h[:, j * 1024:(j + 1) * 1024],
                scale), r=[k], w=[k])
    raT, raT_key = c.arena.alloc([128, KD * 8], BF16)
    maT, maT_key = c.arena.alloc([128, KD * 16], BF16)
    rgT, rgT_key = c.arena.alloc([128, KD], BF16)
    mgT, mgT_key = c.arena.alloc([128, KD], BF16)
    for (dst, dk, wn, nn) in ((raT, raT_key, "L2_attn_reg_addr_w", 8),
                              (maT, maT_key, "L2_attn_mem_addr_w", 16),
                              (rgT, rgT_key, "L2_attn_reg_gate_w", 1),
                              (mgT, mgT_key, "L2_attn_mem_gate_w", 1)):
        for j in range(KD):
            src = W[wn][:, j * 128:(j + 1) * 128].transpose([1, 0])
            p.dma("pool", dst[:, j * nn:(j + 1) * nn], src, w=[dk],
                  allow_slow_non_contiguous=True)
    rvb, rvb_key = stage_row(c, W["L2_attn_reg_val_b"], 1024)
    rqb, rqb_key = stage_row(c, W["L2_attn_reg_q_b"], 1024)
    mvb, mvb_key = stage_row(c, W["L2_attn_mem_val_b"], 1024)
    mqb, mqb_key = stage_row(c, W["L2_attn_mem_q_b"], 1024)
    rab, rab_key = stage_row(c, W["L2_attn_reg_addr_b"], 8)
    mab, mab_key = stage_row(c, W["L2_attn_mem_addr_b"], 16)
    rgb, rgb_key = stage_row(c, W["L2_attn_reg_gate_b"], 1)
    mgb, mgb_key = stage_row(c, W["L2_attn_mem_gate_b"], 1)
    cbb, cbb_key = stage_row(c, W["L2_attn_comb_b"], 1024)
    for (h, k) in ((rqb, rqb_key), (mqb, mqb_key)):
        p.op("dve", lambda e, h=h: e.tensor_scalar_mul(h, h, scale), r=[k], w=[k])
    lgit, lgk = stage_row(c, W["L2_attn_ln_g"], 1024)
    lbit, lbk = stage_row(c, W["L2_attn_ln_b"], 1024)
    grep_, grep_key = c.arena.alloc([128, 1024], F32)
    brep_, brep_key = c.arena.alloc([128, 1024], F32)
    for (row, rk2, dst, dk) in ((lit := lgit, lgk, grep_, grep_key),
                                (lbit, lbk, brep_, brep_key)):
        for half in range(2):
            bank = c.bank()
            p.op("pe", lambda e, b=bank[0], row=row, half=half: e.matmul(
                b[:, 0:512], c.ones_row[0:1, 0:128],
                row[0:1, half * 512:(half + 1) * 512], start=True, stop=True),
                r=[rk2, c.const_buf], w=[(bank[1], 0, 1 << 30)])
            p.op("dve", lambda e, b=bank[0], dst=dst, half=half: e.tensor_copy(
                dst[:, half * 512:(half + 1) * 512], b[:, 0:512]),
                r=[(bank[1], 0, 1 << 30)], w=[dk])
    tri_dr = nc.inline_tensor(
        np.where(np.arange(128)[:, None] <= np.arange(128)[None, :], 1.0, 0.0
                 ).astype(BF), name="l2a_triu")
    tdm_np = np.where(np.arange(128)[:, None] <= np.arange(128)[None, :],
                      DECAY ** (np.arange(128)[None, :] - np.arange(128)[:, None]),
                      0.0).astype(BF)
    tdm_dr = nc.inline_tensor(tdm_np, name="l2a_tdm")
    dc_np = np.stack([DECAY ** (np.arange(128) + 1.0),
                      DECAY ** (127.0 - np.arange(128))], 1).astype(np.float32)
    dc_dr = nc.inline_tensor(dc_np, name="l2a_dc")
    triu01, tri_key = c.arena.alloc([128, 128], BF16)
    tdm, tdm_key = c.arena.alloc([128, 128], BF16)
    dcol, dcol_key = c.arena.alloc([128, 2], F32)
    p.dma_priv("sp", triu01[:], tri_dr.ap(), w=[tri_key])
    p.dma_priv("sp", tdm[:], tdm_dr.ap(), w=[tdm_key])
    p.dma_priv("sp", dcol[:], dc_dr.ap(), w=[dcol_key])
    dcol1 = dcol[:, 0:1]
    ddesc = dcol[:, 1:2]
    xnt, xnt_key = c.arena.alloc([128, 1024], BF16)
    prT = {}
    for nm in ("rq", "rv", "mq", "mv"):
        prT[nm] = c.arena.alloc([128, KD * 128], BF16)
    rvn, rvn_key = c.arena.alloc([128, 1024], BF16)
    mvn, mvn_key = c.arena.alloc([128, 1024], BF16)
    gm, gm_key = c.arena.alloc([128, 64], F32)
    A_, A_key = c.arena.alloc([128, 8], BF16)
    Am_, Am_key = c.arena.alloc([128, 16], BF16)
    Amd, Amd_key = c.arena.alloc([128, 16], BF16)
    P_, P_key = c.arena.alloc([128, 8], BF16)
    Pm_, Pm_key = c.arena.alloc([128, 16], BF16)
    Pmd, Pmd_key = c.arena.alloc([128, 16], BF16)
    tps, tps_key = c.arena.alloc([16, 5 * 128], BF16)
    G_, G_key = c.arena.alloc([128, 128], BF16)
    Gm_, Gm_key = c.arena.alloc([128, 128], BF16)
    M2, M2_key = c.arena.alloc([128, 128], BF16)
    M2m, M2m_key = c.arena.alloc([128, 128], BF16)
    rrT, rrT_key = c.arena.alloc([128, KD * 128], BF16)
    mrT, mrT_key = c.arena.alloc([128, KD * 128], BF16)
    cp, cp_key = c.arena.alloc([128, 1024], F32)
    Sr, Sr_key = c.arena.alloc([8, 1024], F32)
    Sm, Sm_key = c.arena.alloc([16, 1024], F32)
    Srb, Srb_key = c.arena.alloc([8, 1024], BF16)
    Smb, Smb_key = c.arena.alloc([16, 1024], BF16)
    SrT, SrT_key = c.arena.alloc([128, KD * 8], F32)
    SmT, SmT_key = c.arena.alloc([128, KD * 16], F32)
    SrTb, SrTb_key = c.arena.alloc([128, KD * 8], BF16)
    SmTb, SmTb_key = c.arena.alloc([128, KD * 16], BF16)
    xin_t = xin.rearrange("(t p) c -> t p c", p=128)
    xout_t = xout.rearrange("(t p) c -> t p c", p=128)

    def proj_T(dst, dst_key, wname, brow, brow_key):
        h, k = wT[wname]
        for dd in range(KD):
            bank = c.bank()
            mm(c, bank,
               [(h[:, j * 1024 + dd * 128: j * 1024 + (dd + 1) * 128],
                 xnt[:, j * 128:(j + 1) * 128]) for j in range(KD)]
               + [(brow[0:1, dd * 128:(dd + 1) * 128], c.ones_row[0:1, 0:128])],
               [k, xnt_key, brow_key, c.const_buf], n_hi=128)
            p.op("dve", lambda e, b=bank[0], dd=dd: e.tensor_copy(
                dst[:, dd * 128:(dd + 1) * 128], b[:, 0:128]),
                r=[(bank[1], 0, 1 << 30)], w=[dst_key])

    def proj_N(dst, dst_key, wname, brow, brow_key):
        h, k = wT[wname]
        for half in range(2):
            bank = c.bank()
            mm(c, bank,
               [(xnt[:, j * 128:(j + 1) * 128],
                 h[:, j * 1024 + half * 512: j * 1024 + (half + 1) * 512])
                for j in range(KD)]
               + [(c.ones_row[0:1, 0:128], brow[0:1, half * 512:(half + 1) * 512])],
               [k, xnt_key, brow_key, c.const_buf])
            p.op("dve", lambda e, b=bank[0], half=half: e.tensor_copy(
                dst[:, half * 512:(half + 1) * 512], b[:]),
                r=[(bank[1], 0, 1 << 30)], w=[dst_key])

    for sq in range(BPC):
        for (hh, kk2) in ((Sr, Sr_key), (Sm, Sm_key), (Srb, Srb_key), (Smb, Smb_key),
                          (SrT, SrT_key), (SmT, SmT_key), (SrTb, SrTb_key),
                          (SmTb, SmTb_key)):
            p.op("pool", lambda e, hh=hh: e.memset(hh[:], 0.0), w=[kk2])
        for tl in range(NTS):
            t = sq * NTS + tl
            ring = t % 4
            xt_h, xt_b = c.xt[ring], c.xt_buf[ring]
            p.dma("sp", xt_h[:], xin_t[t], r=[(xin_buf, t, t + 1)], w=[xt_b])
            xn_h, xn_b = c.xn[ring], c.xn_buf[ring]
            ln_tile(c, xt_h, xt_b, xn_h, xn_b, s0=4 * ring)
            transpose_128(c, xnt[:].rearrange("p (j q) -> p j q", q=128), xnt_key,
                          xn_h[:], [xn_b])
            proj_T(prT["rq"][0], prT["rq"][1], "rq", rqb, rqb_key)
            proj_T(prT["rv"][0], prT["rv"][1], "rv", rvb, rvb_key)
            proj_T(prT["mq"][0], prT["mq"][1], "mq", mqb, mqb_key)
            proj_T(prT["mv"][0], prT["mv"][1], "mv", mvb, mvb_key)
            proj_N(rvn, rvn_key, "rv", rvb, rvb_key)
            proj_N(mvn, mvn_key, "mv", mvb, mvb_key)
            bank_a = c.bank()
            mm(c, bank_a,
               [(xnt[:, j * 128:(j + 1) * 128], raT[:, j * 8:(j + 1) * 8])
                for j in range(KD)]
               + [(c.ones_row[0:1, 0:128], rab)],
               [raT_key, xnt_key, rab_key, c.const_buf], n_hi=8)
            bank_m = c.bank()
            mm(c, bank_m,
               [(xnt[:, j * 128:(j + 1) * 128], maT[:, j * 16:(j + 1) * 16])
                for j in range(KD)]
               + [(c.ones_row[0:1, 0:128], mab)],
               [maT_key, xnt_key, mab_key, c.const_buf], n_hi=16)
            bank_g = c.bank()
            mm(c, bank_g,
               [(xnt[:, j * 128:(j + 1) * 128], rgT[:, j:j + 1]) for j in range(KD)]
               + [(c.ones_row[0:1, 0:128], rgb)],
               [rgT_key, xnt_key, rgb_key, c.const_buf], n_lo=0, n_hi=1)
            mm(c, bank_g,
               [(xnt[:, j * 128:(j + 1) * 128], mgT[:, j:j + 1]) for j in range(KD)]
               + [(c.ones_row[0:1, 0:128], mgb)],
               [mgT_key, xnt_key, mgb_key, c.const_buf], n_lo=1, n_hi=2)
            gmk = lambda lo, hi: (c.arena.buf, gm_key[1] + lo * 4, gm_key[1] + hi * 4)
            p.op("act", lambda e, b=bank_g[0]: e.activation(
                gm[:, 0:2], b[:, 0:2], AF.Sigmoid, scale=SHARP),
                r=[(bank_g[1], 0, 1 << 30)], w=[gmk(0, 2)])
            p.op("dve", lambda e, b=bank_a[0]: e.reduce_max(
                gm[:, 2:3], b[:, 0:8], axis=AX.X, negate=True),
                r=[(bank_a[1], 0, 1 << 30)], w=[gmk(2, 3)])
            p.op("act", lambda e, b=bank_a[0]: e.activation(
                gm[:, 8:16], b[:, 0:8], AF.Exp, bias=gm[:, 2:3],
                accum_out=gm[:, 3:4]),
                r=[(bank_a[1], 0, 1 << 30), gmk(2, 3)], w=[gmk(3, 16)])
            p.op("dve", lambda e: e.reciprocal(gm[:, 4:5], gm[:, 3:4]),
                 r=[gmk(3, 4)], w=[gmk(4, 5)])
            p.op("dve", lambda e: e.tensor_scalar(
                A_[:], gm[:, 8:16], gm[:, 4:5], gm[:, 0:1], ALU.mult, ALU.mult),
                r=[gmk(0, 16)], w=[A_key])
            p.op("dve", lambda e, b=bank_m[0]: e.reduce_max(
                gm[:, 20:21], b[:, 0:16], axis=AX.X, negate=True),
                r=[(bank_m[1], 0, 1 << 30)], w=[gmk(20, 21)])
            p.op("act", lambda e, b=bank_m[0]: e.activation(
                gm[:, 24:40], b[:, 0:16], AF.Exp, bias=gm[:, 20:21],
                accum_out=gm[:, 21:22]),
                r=[(bank_m[1], 0, 1 << 30), gmk(20, 21)], w=[gmk(21, 40)])
            p.op("dve", lambda e: e.reciprocal(gm[:, 22:23], gm[:, 21:22]),
                 r=[gmk(21, 22)], w=[gmk(22, 23)])
            p.op("dve", lambda e: e.tensor_scalar(
                Am_[:], gm[:, 24:40], gm[:, 22:23], gm[:, 1:2], ALU.mult, ALU.mult),
                r=[gmk(0, 40)], w=[Am_key])
            p.op("dve", lambda e: e.tensor_scalar_mul(Amd[:], Am_[:], ddesc),
                 r=[Am_key, dcol_key], w=[Amd_key])
            bank = c.bank()
            mm(c, bank, [(prT["rv"][0][:, j * 128:(j + 1) * 128],
                          prT["rq"][0][:, j * 128:(j + 1) * 128]) for j in range(KD)],
               [prT["rv"][1], prT["rq"][1]], n_hi=128)
            p.op("dve", lambda e, b=bank[0]: e.scalar_tensor_tensor(
                G_[:], b[:, 0:128], 1.0, triu01[:], ALU.mult, ALU.mult),
                r=[(bank[1], 0, 1 << 30), tri_key], w=[G_key])
            bank = c.bank()
            mm(c, bank, [(prT["mv"][0][:, j * 128:(j + 1) * 128],
                          prT["mq"][0][:, j * 128:(j + 1) * 128]) for j in range(KD)],
               [prT["mv"][1], prT["mq"][1]], n_hi=128)
            p.op("dve", lambda e, b=bank[0]: e.scalar_tensor_tensor(
                Gm_[:], b[:, 0:128], 1.0, tdm[:], ALU.mult, ALU.mult),
                r=[(bank[1], 0, 1 << 30), tdm_key], w=[Gm_key])
            bank_s = c.bank()
            mm(c, bank_s,
               [(G_[:], A_[:])]
               + [(prT["rq"][0][:, dd * 128:(dd + 1) * 128],
                   SrTb[:, dd * 8:(dd + 1) * 8]) for dd in range(KD)],
               [G_key, A_key, prT["rq"][1], SrTb_key], n_hi=8)
            p.op("dve", lambda e, b=bank_s[0]: e.reduce_max(
                gm[:, 40:41], b[:, 0:8], axis=AX.X, negate=True),
                r=[(bank_s[1], 0, 1 << 30)], w=[gmk(40, 41)])
            p.op("act", lambda e, b=bank_s[0]: e.activation(
                gm[:, 48:56], b[:, 0:8], AF.Exp, bias=gm[:, 40:41],
                accum_out=gm[:, 41:42]),
                r=[(bank_s[1], 0, 1 << 30), gmk(40, 41)], w=[gmk(41, 56)])
            p.op("dve", lambda e: e.reciprocal(gm[:, 42:43], gm[:, 41:42]),
                 r=[gmk(41, 42)], w=[gmk(42, 43)])
            p.op("dve", lambda e: e.tensor_scalar_mul(
                P_[:], gm[:, 48:56], gm[:, 42:43]), r=[gmk(40, 56)], w=[P_key])
            bank_i = c.bank()
            mm(c, bank_i, [(Gm_[:], Am_[:])], [Gm_key, Am_key], n_hi=16)
            bank_t = c.bank()
            mm(c, bank_t,
               [(prT["mq"][0][:, dd * 128:(dd + 1) * 128],
                 SmTb[:, dd * 16:(dd + 1) * 16]) for dd in range(KD)],
               [prT["mq"][1], SmTb_key], n_hi=16)
            p.op("dve", lambda e, b=bank_t[0]: e.tensor_scalar_mul(
                gm[:, 48:64], b[:, 0:16], dcol1),
                r=[(bank_t[1], 0, 1 << 30), dcol_key], w=[gmk(48, 64)])
            p.op("dve", lambda e, b=bank_i[0]: e.tensor_tensor(
                gm[:, 48:64], b[:, 0:16], gm[:, 48:64], ALU.add),
                r=[(bank_i[1], 0, 1 << 30), gmk(48, 64)], w=[gmk(48, 64)])
            p.op("dve", lambda e: e.reduce_max(
                gm[:, 20:21], gm[:, 48:64], axis=AX.X, negate=True),
                r=[gmk(48, 64)], w=[gmk(20, 21)])
            p.op("act", lambda e: e.activation(
                gm[:, 24:40], gm[:, 48:64], AF.Exp, bias=gm[:, 20:21],
                accum_out=gm[:, 21:22]),
                r=[gmk(20, 21), gmk(48, 64)], w=[gmk(21, 40)])
            p.op("dve", lambda e: e.reciprocal(gm[:, 22:23], gm[:, 21:22]),
                 r=[gmk(21, 22)], w=[gmk(22, 23)])
            p.op("dve", lambda e: e.tensor_scalar_mul(
                Pm_[:], gm[:, 24:40], gm[:, 22:23]), r=[gmk(20, 40)], w=[Pm_key])
            p.op("dve", lambda e: e.tensor_scalar_mul(Pmd[:], Pm_[:], dcol1),
                 r=[Pm_key, dcol_key], w=[Pmd_key])
            bank_tp, bank_tp_buf = c.bank()
            bbt = bank_tp.bitcast(BF16)

            def fTP(e, bbt=bbt):
                e.transpose(bbt[0:8, 0:128], P_[:], c.identh[:])
                e.transpose(bbt[0:8, 128:256], A_[:], c.identh[:])
                e.transpose(bbt[0:16, 256:384], Pm_[:], c.identh[:])
                e.transpose(bbt[0:16, 384:512], Am_[:], c.identh[:])
                return e.transpose(bbt[0:16, 512:640], Pmd[:], c.identh[:])
            p.op("pe", fTP, r=[P_key, A_key, Pm_key, Am_key, Pmd_key, c.const_buf],
                 w=[(bank_tp_buf, 0, 1 << 30)])
            p.op("dve", lambda e, bbt=bbt: e.tensor_copy(
                tps[0:16, 0:640], bbt[0:16, 0:640]),
                r=[(bank_tp_buf, 0, 1 << 30)], w=[tps_key])
            bank = c.bank()
            mm(c, bank, [(tps[0:8, 128:256], tps[0:8, 0:128])],
               [tps_key], n_hi=128)
            p.op("dve", lambda e, b=bank[0]: e.scalar_tensor_tensor(
                M2[:], b[:, 0:128], 1.0, triu01[:], ALU.mult, ALU.mult),
                r=[(bank[1], 0, 1 << 30), tri_key], w=[M2_key])
            bank = c.bank()
            mm(c, bank, [(tps[0:16, 384:512], tps[0:16, 256:384])],
               [tps_key], n_hi=128)
            p.op("dve", lambda e, b=bank[0]: e.scalar_tensor_tensor(
                M2m[:], b[:, 0:128], 1.0, tdm[:], ALU.mult, ALU.mult),
                r=[(bank[1], 0, 1 << 30), tdm_key], w=[M2m_key])
            for dd in range(KD):
                bank = c.bank()

                def fR(e, b=bank[0], dd=dd):
                    e.matmul(b[:, 0:128], Srb[0:8, dd * 128:(dd + 1) * 128],
                             tps[0:8, 0:128], start=True, stop=False)
                    return e.matmul(b[:, 0:128], rvn[:, dd * 128:(dd + 1) * 128],
                                    M2[:], start=False, stop=True)
                p.op("pe", fR, r=[Srb_key, tps_key, rvn_key, M2_key],
                     w=[(bank[1], 0, 1 << 30)])
                p.op("dve", lambda e, b=bank[0], dd=dd: e.tensor_copy(
                    rrT[:, dd * 128:(dd + 1) * 128], b[:, 0:128]),
                    r=[(bank[1], 0, 1 << 30)], w=[rrT_key])
                bank = c.bank()

                def fM(e, b=bank[0], dd=dd):
                    e.matmul(b[:, 0:128], Smb[0:16, dd * 128:(dd + 1) * 128],
                             tps[0:16, 512:640], start=True, stop=False)
                    return e.matmul(b[:, 0:128], mvn[:, dd * 128:(dd + 1) * 128],
                                    M2m[:], start=False, stop=True)
                p.op("pe", fM, r=[Smb_key, tps_key, mvn_key, M2m_key],
                     w=[(bank[1], 0, 1 << 30)])
                p.op("dve", lambda e, b=bank[0], dd=dd: e.tensor_copy(
                    mrT[:, dd * 128:(dd + 1) * 128], b[:, 0:128]),
                    r=[(bank[1], 0, 1 << 30)], w=[mrT_key])
            for half in range(2):
                bank = c.bank()
                pairs = [(xnt[:, j * 128:(j + 1) * 128],
                          wT["wh"][0][:, j * 1024 + half * 512: j * 1024 + (half + 1) * 512])
                         for j in range(KD)]
                pairs.append((c.ones_row[0:1, 0:128],
                              cbb[0:1, half * 512:(half + 1) * 512]))
                pairs += [(rrT[:, dd * 128:(dd + 1) * 128],
                           wT["wr"][0][:, dd * 1024 + half * 512: dd * 1024 + (half + 1) * 512])
                          for dd in range(KD)]
                pairs += [(mrT[:, dd * 128:(dd + 1) * 128],
                           wT["wm"][0][:, dd * 1024 + half * 512: dd * 1024 + (half + 1) * 512])
                          for dd in range(KD)]
                mm(c, bank, pairs,
                   [xnt_key, wT["wh"][1], wT["wr"][1], wT["wm"][1], rrT_key,
                    mrT_key, cbb_key, c.const_buf])
                p.op("dve", lambda e, b=bank[0], half=half, xn_h=xn_h: e.tensor_tensor(
                    cp[:, half * 512:(half + 1) * 512], b[:],
                    xn_h[:, half * 512:(half + 1) * 512], ALU.add),
                    r=[(bank[1], 0, 1 << 30), xn_b], w=[cp_key])
            sum_ap, sum_k = c.stat_slot(16)
            nm_ap, nm_k = c.stat_slot(17)
            var_ap, var_k = c.stat_slot(18)
            rs_ap, rs_k = c.stat_slot(19)
            p.op("dve", lambda e: e.reduce_sum(sum_ap, cp[:], axis=AX.X),
                 r=[cp_key], w=[sum_k])
            p.op("dve", lambda e: e.tensor_scalar_mul(nm_ap, sum_ap, -1.0 / D),
                 r=[sum_k], w=[nm_k])
            p.op("act", lambda e: e.activation(c.sq[:, 0:D], cp[:], AF.Square,
                                               bias=nm_ap, accum_out=var_ap),
                 r=[cp_key, nm_k], w=[c.sq_buf, var_k])
            eps_ap2 = c.constf(1e-5)
            p.op("act", lambda e: e.activation(rs_ap, var_ap, AF.Sqrt,
                                               scale=1.0 / D, bias=eps_ap2),
                 r=[var_k, c.const_buf], w=[rs_k])
            p.op("dve", lambda e: e.reciprocal(rs_ap, rs_ap), r=[rs_k], w=[rs_k])
            p.op("dve", lambda e: e.tensor_scalar(cp[:], cp[:], nm_ap, rs_ap,
                                                  ALU.add, ALU.mult),
                 r=[cp_key, nm_k, rs_k], w=[cp_key])
            p.op("dve", lambda e: e.tensor_mul(cp[:], cp[:], grep_[:]),
                 r=[cp_key, grep_key], w=[cp_key])
            p.op("dve", lambda e: e.tensor_add(cp[:], cp[:], brep_[:]),
                 r=[cp_key, brep_key], w=[cp_key])
            p.op("dve", lambda e, xt=xt_h: e.tensor_add(xt[:], xt[:], cp[:]),
                 r=[xt_b, cp_key], w=[xt_b])
            p.dma("act", xout_t[t], xt_h[:], r=[xt_b], w=[(xout_buf, t, t + 1)])
            for half in range(2):
                bank = c.bank()
                mm(c, bank, [(A_[:], rvn[:, half * 512:(half + 1) * 512])],
                   [A_key, rvn_key], m_out=8)
                p.op("dve", lambda e, b=bank[0], half=half: e.tensor_tensor(
                    Sr[0:8, half * 512:(half + 1) * 512], b[0:8, 0:512],
                    Sr[0:8, half * 512:(half + 1) * 512], ALU.add),
                    r=[(bank[1], 0, 1 << 30), Sr_key], w=[Sr_key])
                bank = c.bank()
                mm(c, bank, [(Amd[:], mvn[:, half * 512:(half + 1) * 512])],
                   [Amd_key, mvn_key], m_out=16)
                p.op("dve", lambda e, b=bank[0], half=half: e.scalar_tensor_tensor(
                    Sm[0:16, half * 512:(half + 1) * 512],
                    Sm[0:16, half * 512:(half + 1) * 512], DECAY ** 128,
                    b[0:16, 0:512], ALU.mult, ALU.add),
                    r=[(bank[1], 0, 1 << 30), Sm_key], w=[Sm_key])
            p.op("dve", lambda e: e.tensor_copy(Srb[:], Sr[:]), r=[Sr_key], w=[Srb_key])
            p.op("dve", lambda e: e.tensor_copy(Smb[:], Sm[:]), r=[Sm_key], w=[Smb_key])
            bank_r2 = c.bank()

            def fSrT(e, b=bank_r2[0]):
                last = None
                for dd in range(KD):
                    last = e.matmul(b[:, dd * 8:(dd + 1) * 8],
                                    rvn[:, dd * 128:(dd + 1) * 128], A_[:],
                                    start=True, stop=True)
                return last
            p.op("pe", fSrT, r=[rvn_key, A_key], w=[(bank_r2[1], 0, 1 << 30)])
            p.op("dve", lambda e, b=bank_r2[0]: e.tensor_tensor(
                SrT[:], b[:, 0:64], SrT[:], ALU.add),
                r=[(bank_r2[1], 0, 1 << 30), SrT_key], w=[SrT_key])
            p.op("dve", lambda e: e.tensor_copy(SrTb[:], SrT[:]),
                 r=[SrT_key], w=[SrTb_key])
            bank_m2 = c.bank()

            def fSmT(e, b=bank_m2[0]):
                last = None
                for dd in range(KD):
                    last = e.matmul(b[:, dd * 16:(dd + 1) * 16],
                                    mvn[:, dd * 128:(dd + 1) * 128], Amd[:],
                                    start=True, stop=True)
                return last
            p.op("pe", fSmT, r=[mvn_key, Amd_key], w=[(bank_m2[1], 0, 1 << 30)])
            p.op("dve", lambda e, b=bank_m2[0]: e.scalar_tensor_tensor(
                SmT[:], SmT[:], DECAY ** 128, b[:, 0:128], ALU.mult, ALU.add),
                r=[(bank_m2[1], 0, 1 << 30), SmT_key], w=[SmT_key])
            p.op("dve", lambda e: e.tensor_copy(SmTb[:], SmT[:]),
                 r=[SmT_key], w=[SmTb_key])


SEG_BUILDERS = {}
SEG_WEIGHTS = {}


def _reg(name, weights):
    def deco(fn):
        SEG_BUILDERS[name] = fn
        SEG_WEIGHTS[name] = weights
        return fn
    return deco


SEG_BUILDERS["L1_ffn"] = seg_l1_ffn
SEG_WEIGHTS["L1_ffn"] = ["L1_ffn_w_gate", "L1_ffn_w_up", "L1_ffn_w_down"]
SEG_BUILDERS["L2_attn"] = seg_l2_attn
SEG_WEIGHTS["L2_attn"] = ["L2_attn_" + x for x in [
    "reg_gate_w", "reg_gate_b", "reg_addr_w", "reg_addr_b", "reg_val_w",
    "reg_val_b", "reg_q_w", "reg_q_b", "mem_gate_w", "mem_gate_b",
    "mem_addr_w", "mem_addr_b", "mem_val_w", "mem_val_b", "mem_q_w",
    "mem_q_b", "comb_w", "comb_b", "ln_g", "ln_b"]]
SEG_BUILDERS["L0_ffn"] = seg_l0_ffn
SEG_WEIGHTS["L0_ffn"] = ["L0_ffn_Wqkv", "L0_ffn_q_scale", "L0_ffn_k_scale",
                         "L0_ffn_Wo", "L0_ffn_ln_g", "L0_ffn_ln_b",
                         "L0_ffn_W1", "L0_ffn_b1", "L0_ffn_W2", "L0_ffn_b2"]
SEG_BUILDERS["L2_ffn"] = seg_l2_ffn
SEG_WEIGHTS["L2_ffn"] = ["L2_ffn_wqkv", "L2_ffn_q_scale", "L2_ffn_k_scale",
                         "L2_ffn_wo", "L2_ffn_ln_g", "L2_ffn_ln_b",
                         "L2_ffn_w1", "L2_ffn_b1", "L2_ffn_w2", "L2_ffn_b2"]
SEG_BUILDERS["L1_attn"] = seg_l1_attn
SEG_WEIGHTS["L1_attn"] = ["L1_attn_w_attn", "L1_attn_w_proj"]
SEG_BUILDERS["L3_attn"] = seg_l3_attn
SEG_WEIGHTS["L3_attn"] = ["L3_attn_W_basis", "L3_attn_W_v", "L3_attn_b_v",
                          "L3_attn_W_o", "L3_attn_b_o"]
SEG_BUILDERS["L0_attn"] = seg_l0_attn
SEG_WEIGHTS["L0_attn"] = ["L0_attn_wq", "L0_attn_bq", "L0_attn_wu", "L0_attn_bu"]
SEG_BUILDERS["L3_ffn"] = seg_l3_ffn
SEG_WEIGHTS["L3_ffn"] = ["L3_ffn_Wr", "L3_ffn_br", "L3_ffn_W1", "L3_ffn_b1",
                         "L3_ffn_W2", "L3_ffn_b2"]

SEG_ORDER = ["L0_attn", "L0_ffn", "L1_attn", "L1_ffn",
             "L2_attn", "L2_ffn", "L3_attn", "L3_ffn"]

INPUT_SPECS = {}


def _input_dtype(name, arr):
    if arr.dtype == np.int32:
        return I32
    return F32


def build_segment_nc(seg_name, weight_arrays):
    import os
    nc = bass.Bass("TRN2", target_bir_lowering=False)
    xi = nc.dram_tensor("xi", [T, D], F32, kind="ExternalInput")
    xo = nc.dram_tensor("xo_out", [T, D], F32, kind="ExternalOutput")
    global _DBG
    _DBG = None
    if os.environ.get("KDBG"):
        _DBG = (nc.dram_tensor("dbg_out", [16, 128, 1024], F32,
                               kind="ExternalOutput").ap(), Buf("dbg"))
    W = {}
    for wname in SEG_WEIGHTS[seg_name]:
        arr = weight_arrays[wname]
        W[wname] = nc.dram_tensor(wname, list(arr.shape),
                                  _input_dtype(wname, arr),
                                  kind="ExternalInput").ap()
    c = Ctx(nc)
    xin_buf = Buf("xin")
    xout_buf = Buf("xout")
    SEG_BUILDERS[seg_name](c, xi.ap(), xin_buf, xo.ap(), xout_buf, W)
    c.p.op("dve", lambda e: e.engine_nop(), r=[xout_buf])
    c.p.emit()
    return nc


def _implemented_segments():
    return [s for s in SEG_ORDER if s in SEG_BUILDERS]


def kernel(**inputs):
    x = np.ascontiguousarray(np.asarray(inputs["L0_attn_x"], dtype=np.float32))
    shards = [np.ascontiguousarray(x[2 * i:2 * i + 2].reshape(T, D)) for i in range(NCORE)]
    for seg in _implemented_segments():
        wnames = SEG_WEIGHTS[seg]
        warrs = {n: np.ascontiguousarray(np.asarray(inputs[n])) for n in wnames}
        nc = build_segment_nc(seg, warrs)
        in_maps = []
        for i in range(NCORE):
            m = {"xi": shards[i]}
            m.update(warrs)
            in_maps.append(m)
        res = run_bass_kernel_spmd(nc, in_maps, core_ids=list(range(NCORE)))
        import os as _os
        if _os.environ.get("KDBG"):
            global DBG_OUT
            DBG_OUT = np.asarray(res.results[0].get("dbg_out"))
        shards = [np.asarray(res.results[i]["xo_out"]) for i in range(NCORE)]
    out = np.stack([s.reshape(BPC, S, D) for s in shards]).reshape(B := 16, S, D)
    return out
